# version 01
# baseline (speedup 1.0000x reference)
import contextlib

import numpy as np
import ml_dtypes

import concourse.bass as bass
import concourse.mybir as mybir
from concourse.bass_utils import run_bass_kernel_spmd

F32 = mybir.dt.float32
BF16 = mybir.dt.bfloat16
I32 = mybir.dt.int32
AF = mybir.ActivationFunctionType
ALU = mybir.AluOpType
AXX = mybir.AxisListType.X

P = 128
D = 1024
D3 = 3072
CH = 256
S_TOT = 18
HALF1 = list(range(0, 10))
HALF2 = list(range(10, 18))
EPS = 1e-5
BIG = 1024.0

_DEBUG = False


def _dft_tables():
    n = 256
    j = np.arange(n)[:, None].astype(np.float64)
    k = np.arange(128)[None, :].astype(np.float64)
    ang = 2.0 * np.pi * (j * k) / n
    fwd_re = np.cos(ang)
    fwd_imp = -np.sin(ang)
    fwd_imp[:, 0] = np.cos(np.pi * j[:, 0])
    tau = np.arange(128)[None, :].astype(np.float64)
    jj = 128.0 + tau
    kk = np.arange(128)[:, None].astype(np.float64)
    c = np.cos(2.0 * np.pi * kk * jj / n) * (2.0 / n)
    c[0, :] = 1.0 / n
    s = -np.sin(2.0 * np.pi * kk * jj / n) * (2.0 / n)
    s[0, :] = np.cos(np.pi * jj[0]) * (1.0 / n)
    bf = ml_dtypes.bfloat16
    return (np.ascontiguousarray(fwd_re.reshape(2, 128, 128).transpose(1, 0, 2)).astype(bf),
            np.ascontiguousarray(fwd_imp.reshape(2, 128, 128).transpose(1, 0, 2)).astype(bf),
            c.astype(bf), s.astype(bf))


class B:
    ENGS = ("tensor", "vector", "scalar", "gpsimd", "sync")

    def __init__(self):
        self.streams = {e: [] for e in self.ENGS}
        self.cnt = {e: 0 for e in self.ENGS}

    def op(self, eng, fn, waits=(), inc=True):
        step = 16 if eng == "sync" else 1
        if inc:
            self.cnt[eng] += step
        self.streams[eng].append((fn, tuple(waits), inc, self.cnt[eng]))
        return self.cnt[eng]

    def t(self, fn, waits=(), inc=True):
        return self.op("tensor", fn, waits, inc)

    def v(self, fn, waits=(), inc=True):
        return self.op("vector", fn, waits, inc)

    def s(self, fn, waits=(), inc=True):
        return self.op("scalar", fn, waits, inc)

    def g(self, fn, waits=(), inc=True):
        return self.op("gpsimd", fn, waits, inc)

    def d(self, fn, waits=(), inc=True):
        return self.op("sync", fn, waits, inc)

    def run(self, block, sems):
        def make_body(eng_name):
            stream = self.streams[eng_name]
            step = 16 if eng_name == "sync" else 1

            def body(h):
                waited = {}
                for fn, waits, inc, val in stream:
                    for we, wv in waits:
                        if waited.get(we, -1) < wv:
                            h.wait_ge(sems[we], wv)
                            waited[we] = wv
                    inst = fn(h)
                    if inc:
                        if inst is None:
                            inst = h.nop()
                        inst.then_inc(sems[eng_name], step)
            return body

        block.tensor(make_body("tensor"))
        block.vector(make_body("vector"))
        block.scalar(make_body("scalar"))
        block.gpsimd(make_body("gpsimd"))
        block.sync(make_body("sync"))


def build_nc(debug=False):
    nc = bass.Bass("TRN2", target_bir_lowering=False)
    di = {}

    def din(name, shape, dt=F32):
        di[name] = nc.dram_tensor(name, shape, dt, kind="ExternalInput").ap()
        return di[name]

    din("xin", [S_TOT, P, D])
    din("halo_mask", [P, 2])
    din("W_in", [8, P, D3])
    din("W_out", [8, P, D])
    din("Cw1", [8, P, CH])
    din("Cw2", [2, P, 2])
    din("sigs_raw", [64, D])
    din("k0c", [8, P, P])
    din("k1c", [8, P, P])
    din("gamma_cols", [P, 8])
    din("beta_rep", [P, 8, P], BF16)
    din("b_in_rep", [P, D3])
    din("b_out_rep", [P, D])
    din("cb1_row", [1, CH], BF16)
    din("cb2_rep", [P, 2])
    din("gf_rep", [P, D])
    din("bf_rep", [P, D])
    din("dirs", [64, D])
    din("ptab", [64, 65])
    din("knots_rep", [P, 8])
    din("temp_rep", [P, 1])
    din("osc_rep", [P, 1])
    din("fwd_re", [P, 2, P], BF16)
    din("fwd_imp", [P, 2, P], BF16)
    din("inv_c", [P, P], BF16)
    din("inv_s", [P, P], BF16)
    din("ident", [P, P], BF16)
    din("io64", [P, 64])
    din("io8B", [P, 8])
    din("ones_row", [1, P], BF16)

    xout = nc.dram_tensor("xout", [16, P, D], F32, kind="ExternalOutput").ap()

    dbg = {}
    if debug:
        for nm, shape, dt_ in [("d_scores", [10, P, 72], F32), ("d_sig", [10, P, 16], F32),
                               ("d_jsel", [10, P, 1], F32), ("d_mag", [10, P, 1], F32),
                               ("d_x1", [10, P, D], F32), ("d_v", [3, P, D], BF16),
                               ("d_g1", [3, P, D], BF16), ("d_u", [3, P, D], BF16),
                               ("d_yg", [3, P, D], BF16), ("d_x2", [10, P, D], F32),
                               ("d_comp", [10, P, 2], F32), ("d_pars", [10, P, 65], F32),
                               ("d_cols", [P, 160], F32), ("d_gates", [10, P, 16], F32),
                               ("d_rcol", [10, P, 8], F32), ("d_ybuf", [P, D], F32),
                               ("d_lnb", [P, D], BF16), ("d_tb", [P, D], BF16),
                               ("d_colY", [P, 8], F32)]:
            dbg[nm] = nc.dram_tensor(nm, shape, dt_, kind="ExternalOutput").ap()

    b = B()

    with contextlib.ExitStack() as ctx:
        def sb(name, shape, dt):
            return ctx.enter_context(nc.sbuf_tensor(name, shape, dt))

        x = sb("x", [P, 10, D], F32)
        winp = sb("winp", [P, 8, D3], BF16)
        wout = sb("wout", [P, 8, D], BF16)
        cw1 = sb("cw1", [P, 8, CH], BF16)
        cw2 = sb("cw2", [P, 2, 2], BF16)
        sigc = sb("sigc", [P, 8, 72], BF16)
        biasr = sb("biasr", [P, D3], BF16)
        cb1r = sb("cb1r", [1, CH], BF16)
        cb2r = sb("cb2r", [P, 2], F32)
        boutr = sb("boutr", [P, D], F32)
        g05 = sb("g05", [P, D], F32)
        bt05 = sb("bt05", [P, D], F32)
        dirs_b = sb("dirs_b", [64, D], BF16)
        ptab_b = sb("ptab_b", [64, 65], BF16)
        ktab = {kk: dict(A1=sb(f"A1_{kk}", [P, D], BF16),
                         B=sb(f"B_{kk}", [P, D], BF16),
                         A2=sb(f"A2_{kk}", [P, D], BF16)) for kk in (0, 1)}
        fwd_re = sb("fwd_re_s", [P, 2, P], BF16)
        fwd_imp = sb("fwd_imp_s", [P, 2, P], BF16)
        inv_c = sb("inv_c_s", [P, P], BF16)
        inv_s = sb("inv_s_s", [P, P], BF16)
        ident = sb("ident_s", [P, P], BF16)
        io64 = sb("io64_s", [P, 64], F32)
        io8B = sb("io8B_s", [P, 8], F32)
        ones_row = sb("ones_row_s", [1, P], BF16)
        knots = sb("knots_s", [P, 8], F32)
        dknot = sb("dknot_s", [P, 7], F32)
        invt = sb("invt_s", [P, 1], F32)
        oscr = sb("oscr_s", [P, 1], F32)
        maskc = sb("maskc", [P, 2], F32)

        stg = sb("stg", [P, D3 // 2], F32)
        zf = sb("zf", [P, D], F32)
        zb = sb("zb", [P, D], BF16)
        zlo = sb("zlo", [P, D], BF16)
        zhiT = sb("zhiT", [P, 8, P], BF16)
        zloT = sb("zloT", [P, 8, P], BF16)
        hb = sb("hb", [P, CH], BF16)
        hT = sb("hT", [P, 2, P], BF16)
        ohT = sb("ohT", [64, P], BF16)

        NS = 10
        scoresA = sb("scoresA", [P, NS, 72], F32)
        statsA = sb("statsA", [P, NS, 12], F32)
        colA = sb("colA", [P, NS, 16], F32)
        gw0 = sb("gw0", [P, NS, 16], F32)
        gw1 = sb("gw1", [P, NS, 8], F32)
        gw2 = sb("gw2", [P, NS, 8], F32)
        gw3 = sb("gw3", [P, NS, 8], F32)
        gates = sb("gates", [P, NS, 16], F32)
        rcol = sb("rcol", [P, NS, 8], F32)
        onehot = sb("onehot", [P, NS, 64], BF16)
        compA = sb("compA", [P, NS, 2], F32)
        spw = sb("spw", [P, NS, 16], F32)
        spw2 = sb("spw2", [P, NS, 16], F32)
        parsA = sb("parsA", [P, NS, 65], F32)
        ybuf = sb("ybuf", [P, D], F32)

        vring = sb("vring", [P, 2, D], BF16)
        uring = sb("uring", [P, 2, D], BF16)
        g1c = sb("g1c", [P, D], BF16)
        g2c = sb("g2c", [P, D], BF16)
        ygc = sb("ygc", [P, D], BF16)
        fq1 = sb("fq1", [P, D], BF16)
        fq2 = sb("fq2", [P, D], BF16)
        fqt = sb("fqt", [P, D], BF16)

        pp0 = ctx.enter_context(nc.psum_tensor("pp0", [P, D], F32))
        pp1 = ctx.enter_context(nc.psum_tensor("pp1", [P, D], F32))
        pp2 = ctx.enter_context(nc.psum_tensor("pp2", [P, D], F32))
        ppT = ctx.enter_context(nc.psum_tensor("ppT", [P, D], BF16))
        pps = ctx.enter_context(nc.psum_tensor("pps", [P, 512], F32))

        sems = {e: ctx.enter_context(nc.semaphore(f"sem_{e}")) for e in B.ENGS}
        block = ctx.enter_context(nc.Block())

        def W(*pairs):
            return tuple(pairs)

        def vsep():
            b.v(lambda h: h.tensor_scalar(sepA[:], sepA[:], 0.0, None, ALU.mult),
                inc=False)

        def ssep():
            b.s(lambda h: h.activation(sepA[:], sepA[:], AF.Identity), inc=False)


        ld = {}
        stg_free = [0]

        def load(name, out_ap, in_ap, waits=()):
            ld[name] = b.d(lambda h, o=out_ap, i=in_ap: h.dma_start(out=o, in_=i),
                           waits)
            return ld[name]

        def load_stg(name, in_ap, cols, rows=P):
            return load(name, stg[:rows, 0:cols], in_ap, W(("vector", stg_free[0])))

        load("ident", ident[:], di["ident"])
        load("io64", io64[:], di["io64"])
        load("io8B", io8B[:], di["io8B"])
        load("ones_row", ones_row[:], di["ones_row"])
        load("knots", knots[:], di["knots_rep"])
        load("temp", invt[:], di["temp_rep"])
        load("osc", oscr[:], di["osc_rep"])
        load("mask", maskc[:], di["halo_mask"])
        load("fwd_re", fwd_re[:], di["fwd_re"])
        load("fwd_imp", fwd_imp[:], di["fwd_imp"])
        load("inv_c", inv_c[:], di["inv_c"])
        load("inv_s", inv_s[:], di["inv_s"])
        load("cb1", cb1r[:], di["cb1_row"])
        load("cb2", cb2r[:], di["cb2_rep"])
        load("bout", boutr[:], di["b_out_rep"])
        load("beta", zhiT[:], di["beta_rep"])
        x_loaded = {}
        for s in range(10):
            x_loaded[s] = load(f"x{s}", x[:, s, :], di["xin"][s])

        sigs16 = ygc
        load_stg("sigs_raw", di["sigs_raw"], D, rows=64)
        b.v(lambda h: h.tensor_scalar(zf[:64, :], stg[:64, 0:D], 0.3, None, ALU.is_gt),
            W(("sync", ld["sigs_raw"])))
        b.v(lambda h: h.tensor_scalar(ybuf[:64, :], stg[:64, 0:D], -0.3, None, ALU.is_lt))
        b.v(lambda h: h.tensor_sub(zf[:64, :], zf[:64, :], ybuf[:64, :]))
        sig_cast = b.v(lambda h: h.tensor_copy(sigs16[:64, :], zf[:64, :]))
        stg_free[0] = sig_cast
        for c in range(8):
            b.t(lambda h, c=c: h.transpose(ppT[:, c * 64:(c + 1) * 64],
                                           sigs16[:64, c * P:(c + 1) * P],
                                           ident[:64, :64]),
                W(("vector", sig_cast), ("sync", ld["ident"])), inc=(c == 7))
        tp = b.cnt["tensor"]
        for c in range(8):
            b.v(lambda h, c=c: h.tensor_copy(sigc[:, c, 0:64], ppT[:, c * 64:(c + 1) * 64]),
                W(("tensor", tp)))
        cs_v = sb("cs_v", [P, 8, 8], F32)
        cs_w = sb("cs_w", [P, 8, 8], F32)
        b.v(lambda h: h.tensor_reduce(
            cs_v[:],
            sigc[:, :, 0:64].rearrange("p c (g l) -> p c g l", g=8), AXX, ALU.add))
        vsep()
        b.v(lambda h: h.tensor_scalar(cs_w[:], cs_v[:], 0.0, None, ALU.is_gt))
        vsep()
        b.v(lambda h: h.tensor_scalar(cs_v[:], cs_v[:], 0.0, None, ALU.is_lt))
        vsep()
        b.v(lambda h: h.tensor_sub(cs_w[:], cs_w[:], cs_v[:]))
        for c in range(8):
            b.v(lambda h, c=c: h.tensor_copy(sigc[:, c, 64:72], cs_w[:, c, :]))

        def rsqrt_newton(dst, src, k, waits=()):
            b.v(lambda h: h.tensor_copy(rsA[:, 0:k], src), waits)
            vsep()
            vsep()
            b.v(lambda h: h.tensor_scalar(rsB[:].bitcast(I32), rsA[:].bitcast(I32), 1,
                                          None, ALU.logical_shift_right))
            vsep()
            b.v(lambda h: h.tensor_scalar(rsB[:].bitcast(I32), rsB[:].bitcast(I32), -1,
                                          0x5F3759DF, ALU.mult, ALU.add))
            vsep()
            for _ in range(3):
                b.v(lambda h: h.tensor_mul(rsC[:], rsB[:], rsB[:]))
                vsep()
                b.v(lambda h: h.tensor_mul(rsC[:], rsC[:], rsA[:]))
                vsep()
                b.v(lambda h: h.tensor_scalar(rsC[:], rsC[:], -0.5, 1.5, ALU.mult,
                                              ALU.add))
                vsep()
                b.v(lambda h: h.tensor_mul(rsB[:], rsB[:], rsC[:]))
                vsep()
            b.v(lambda h: h.tensor_copy(dst, rsB[:, 0:k]))
            vsep()

        tc1 = sb("tc1", [P, 4], F32)
        rsA = sb("rsA", [P, 128], F32)
        rsB = sb("rsB", [P, 128], F32)
        rsC = sb("rsC", [P, 128], F32)
        sepA = sb("sepA", [P, 64], F32)
        colY = sb("colY", [P, 8], F32)
        b.v(lambda h: h.tensor_scalar(tc1[:, 0:1], invt[:], 1e-6, None, ALU.add),
            W(("sync", ld["temp"])))
        vsep()
        rsqrt_newton(tc1[:, 1:2], tc1[:, 0:1], 1)
        vsep()
        b.v(lambda h: h.tensor_mul(invt[:], tc1[:, 1:2], tc1[:, 1:2]))
        b.v(lambda h: h.tensor_sub(dknot[:], knots[:, 1:8], knots[:, 0:7]),
            W(("sync", ld["knots"])))

        load_stg("gf", di["gf_rep"], D)
        stg_free[0] = b.v(lambda h: h.tensor_scalar(g05[:], stg[:, 0:D], 0.5, None,
                                                    ALU.mult), W(("sync", ld["gf"])))
        load_stg("bf", di["bf_rep"], D)
        stg_free[0] = b.v(lambda h: h.tensor_scalar(bt05[:], stg[:, 0:D], 0.5, None,
                                                    ALU.mult), W(("sync", ld["bf"])))
        load_stg("dirs_f", di["dirs"], D, rows=64)
        stg_free[0] = b.v(lambda h: h.tensor_copy(dirs_b[:], stg[:64, 0:D]),
                          W(("sync", ld["dirs_f"])))
        load_stg("ptab_f", di["ptab"], 65, rows=64)
        stg_free[0] = b.v(lambda h: h.tensor_copy(ptab_b[:], stg[:64, 0:65]),
                          W(("sync", ld["ptab_f"])))
        for c in range(8):
            load_stg(f"cw1_{c}", di["Cw1"][c], CH)
            stg_free[0] = b.v(lambda h, c=c: h.tensor_copy(cw1[:, c, :], stg[:, 0:CH]),
                              W(("sync", ld[f"cw1_{c}"])))
        for c in range(2):
            load_stg(f"cw2_{c}", di["Cw2"][c], 2)
            stg_free[0] = b.v(lambda h, c=c: h.tensor_copy(cw2[:, c, :], stg[:, 0:2]),
                              W(("sync", ld[f"cw2_{c}"])))
        for hhf in range(2):
            load_stg(f"binr{hhf}",
                     di["b_in_rep"][:, hhf * (D3 // 2):(hhf + 1) * (D3 // 2)],
                     D3 // 2)
            stg_free[0] = b.v(lambda h, hhf=hhf: h.tensor_copy(
                biasr[:, hhf * (D3 // 2):(hhf + 1) * (D3 // 2)], stg[:]),
                W(("sync", ld[f"binr{hhf}"])))
        gcols = sb("gcols", [P, 8], F32)
        load("gcols", gcols[:], di["gamma_cols"])

        H3 = D3 // 2
        for c in range(8):
            for hhf in range(2):
                load_stg(f"win_{c}_{hhf}", di["W_in"][c][:, hhf * H3:(hhf + 1) * H3],
                         H3)
                wv = b.v(lambda h, c=c, hhf=hhf: h.tensor_copy(
                    winp[:, c, hhf * H3:(hhf + 1) * H3], stg[:]),
                    W(("sync", ld[f"win_{c}_{hhf}"]), ("sync", ld["beta"])))
                for n3 in range(3):
                    n6 = hhf * 3 + n3
                    pdst = (pp0, pp1, pp2)[n6 // 2]
                    off = (n6 % 2) * 512
                    b.t(lambda h, c=c, n6=n6, pdst=pdst, off=off: h.matmul(
                        pdst[:, off:off + 512], zhiT[:, c, :],
                        winp[:, c, n6 * 512:(n6 + 1) * 512],
                        start=(c == 0), stop=(c == 7)),
                        W(("vector", wv)), inc=(n3 == 2))
                stg_free[0] = b.v(lambda h, c=c, hhf=hhf: h.tensor_scalar(
                    winp[:, c, hhf * H3:(hhf + 1) * H3], stg[:], gcols[:, c:c + 1],
                    None, ALU.mult),
                    W(("tensor", b.cnt["tensor"]), ("sync", ld["gcols"])))
        bias_mm = b.cnt["tensor"]
        for n6 in range(6):
            pdst = (pp0, pp1, pp2)[n6 // 2]
            off = (n6 % 2) * 512
            b.v(lambda h, n6=n6, pdst=pdst, off=off: h.tensor_add(
                biasr[:, n6 * 512:(n6 + 1) * 512], biasr[:, n6 * 512:(n6 + 1) * 512],
                pdst[:, off:off + 512]), W(("tensor", bias_mm)))
        pp_free = b.cnt["vector"]

        for c in range(8):
            load_stg(f"wo_{c}", di["W_out"][c], D)
            stg_free[0] = b.v(lambda h, c=c: h.tensor_copy(wout[:, c, :], stg[:, 0:D]),
                              W(("sync", ld[f"wo_{c}"])))

        kT = zloT
        kch = zb[:, 0:P]
        for kk, src in ((0, "k0c"), (1, "k1c")):
            for c in range(8):
                load_stg(f"k{kk}_{c}", di[src][c], P)
                cv = b.v(lambda h: h.tensor_copy(kch, stg[:, 0:P]),
                         W(("sync", ld[f"k{kk}_{c}"])))
                stg_free[0] = cv
                b.t(lambda h, c=c: h.transpose(ppT[:, c * P:(c + 1) * P], kch,
                                               ident[:]), W(("vector", cv)))
                b.v(lambda h, c=c: h.tensor_copy(kT[:, c, :], ppT[:, c * P:(c + 1) * P]),
                    W(("tensor", b.cnt["tensor"])))
            kt_done = b.cnt["vector"]
            for n2 in range(2):
                b.t(lambda h, n2=n2: h.matmul(
                    pp0[:, n2 * 512:(n2 + 1) * 512], fwd_re[:, 0, :],
                    kT[:].rearrange("p c k -> p (c k)")[:, n2 * 512:(n2 + 1) * 512],
                    start=True, stop=True),
                    W(("vector", max(kt_done, pp_free)), ("sync", ld["fwd_re"]),
                      ("sync", ld["fwd_imp"])) if n2 == 0 else (), inc=False)
            for n2 in range(2):
                b.t(lambda h, n2=n2: h.matmul(
                    pp1[:, n2 * 512:(n2 + 1) * 512], fwd_imp[:, 0, :],
                    kT[:].rearrange("p c k -> p (c k)")[:, n2 * 512:(n2 + 1) * 512],
                    start=True, stop=True), inc=(n2 == 1))
            tmm = b.cnt["tensor"]
            kt = ktab[kk]
            b.v(lambda h, kt=kt: h.tensor_copy(kt["A1"][:], pp0[:]), W(("tensor", tmm)))
            b.v(lambda h, kt=kt: h.tensor_copy(kt["B"][:], pp1[:]))
            b.v(lambda h, kt=kt: h.tensor_copy(kt["A2"][:], pp0[:]))
            b.v(lambda h, kt=kt: h.tensor_copy(kt["A2"][0:1, :], pp1[0:1, :]))
            pp_free = b.v(lambda h, kt=kt: h.memset(kt["B"][0:1, :], 0.0))

        state = dict(vready={}, uready={}, x2v={})

        def ln_stats(subs, c0, xw=None):
            n = len(subs)
            for i, s in enumerate(subs):
                sl = s % 10
                w = W(("sync", xw[s])) if xw else ()
                b.v(lambda h, i=i, sl=sl: h.bn_stats(statsA[:, i, 0:6],
                                                     x[:, sl, 0:512]), w)
                b.v(lambda h, i=i, sl=sl: h.bn_stats(statsA[:, i, 6:12],
                                                     x[:, sl, 512:D]))
            vsep()
            vsep()
            st4 = statsA[:, 0:n, :].rearrange("p s (t e) -> p s t e", t=4)
            mu4 = st4[:, :, :, 1]
            M4 = st4[:, :, :, 2]
            b.v(lambda h, n=n: h.tensor_reduce(colA[:, 0:n, c0 + 2], mu4, AXX,
                                               ALU.add))
            b.v(lambda h, n=n: h.tensor_mul(gw1[:, 0:n, 0:4], mu4, mu4))
            vsep()
            b.v(lambda h, n=n: h.tensor_reduce(colA[:, 0:n, c0 + 3],
                                               gw1[:, 0:n, 0:4], AXX, ALU.add))
            b.v(lambda h, n=n: h.tensor_reduce(colA[:, 0:n, c0 + 4], M4, AXX,
                                               ALU.add))
            b.v(lambda h, n=n: h.tensor_scalar(colA[:, 0:n, c0 + 2],
                                               colA[:, 0:n, c0 + 2], 0.25, None,
                                               ALU.mult))
            vsep()
            b.v(lambda h, n=n: h.tensor_scalar(colA[:, 0:n, c0 + 3],
                                               colA[:, 0:n, c0 + 3], 0.25, None,
                                               ALU.mult))
            vsep()
            b.v(lambda h, n=n: h.scalar_tensor_tensor(
                colA[:, 0:n, c0 + 3], colA[:, 0:n, c0 + 4], 1.0 / D,
                colA[:, 0:n, c0 + 3], ALU.mult, ALU.add))
            b.v(lambda h, n=n: h.tensor_mul(colA[:, 0:n, c0 + 4],
                                            colA[:, 0:n, c0 + 2],
                                            colA[:, 0:n, c0 + 2]))
            vsep()
            b.v(lambda h, n=n: h.tensor_sub(colA[:, 0:n, c0 + 3],
                                            colA[:, 0:n, c0 + 3],
                                            colA[:, 0:n, c0 + 4]))
            vsep()
            b.v(lambda h, n=n: h.tensor_scalar(colA[:, 0:n, c0 + 3],
                                               colA[:, 0:n, c0 + 3], EPS, None,
                                               ALU.add))
            vsep()
            rsqrt_newton(colA[:, 0:n, c0 + 1], colA[:, 0:n, c0 + 3], n)
            vsep()
            b.v(lambda h, n=n: h.tensor_mul(colA[:, 0:n, c0 + 0], colA[:, 0:n, c0 + 2],
                                            colA[:, 0:n, c0 + 1]))
            vsep()
            b.v(lambda h, n=n: h.tensor_scalar(colA[:, 0:n, c0 + 0],
                                               colA[:, 0:n, c0 + 0], -1.0, None,
                                               ALU.mult))

        def sigma_chain(dst, src, n):
            b.s(lambda h: h.activation(gw1[:, 0:n, :], src, AF.Abs, scale=invt[:]),
                W(("vector", b.cnt["vector"])))
            ssep()
            sS = b.s(lambda h: h.activation(gw1[:, 0:n, :], gw1[:, 0:n, :], AF.Exp,
                                            scale=-1.0))
            b.v(lambda h: h.tensor_scalar(gw1[:, 0:n, :], gw1[:, 0:n, :], 1.0, None,
                                          ALU.add), W(("scalar", sS)))
            vsep()
            b.v(lambda h: h.tensor_scalar(gw2[:, 0:n, :], gw1[:, 0:n, :], -8.0 / 17.0,
                                          24.0 / 17.0, ALU.mult, ALU.add))
            vsep()
            for _ in range(3):
                b.v(lambda h: h.tensor_mul(dst, gw1[:, 0:n, :], gw2[:, 0:n, :]))
                vsep()
                b.v(lambda h: h.tensor_scalar(dst, dst, -1.0, 2.0, ALU.mult, ALU.add))
                vsep()
                b.v(lambda h: h.tensor_mul(gw2[:, 0:n, :], gw2[:, 0:n, :], dst))
                vsep()
            b.v(lambda h: h.tensor_scalar(gw1[:, 0:n, :], src, 0.0, None, ALU.is_lt))
            b.v(lambda h: h.tensor_scalar(dst, gw2[:, 0:n, :], -2.0, 1.0, ALU.mult,
                                          ALU.add))
            vsep()
            b.v(lambda h: h.tensor_mul(gw1[:, 0:n, :], gw1[:, 0:n, :], dst))
            vsep()
            b.v(lambda h: h.tensor_add(dst, gw2[:, 0:n, :], gw1[:, 0:n, :]))

        def calib(dst, sig, n):
            b.v(lambda h: h.tensor_scalar(dst, sig, 0.0, knots[:, 0:1], ALU.mult,
                                          ALU.add))
            for j in range(7):
                b.v(lambda h, j=j: h.tensor_scalar(gw2[:, 0:n, :], sig, 7.0, -float(j),
                                                   ALU.mult, ALU.add))
                vsep()
                b.v(lambda h: h.tensor_scalar(gw2[:, 0:n, :], gw2[:, 0:n, :], 0.0, 1.0,
                                              ALU.max, ALU.min))
                vsep()
                b.v(lambda h, j=j: h.scalar_tensor_tensor(dst, gw2[:, 0:n, :],
                                                          dknot[:, j:j + 1], dst,
                                                          ALU.mult, ALU.add))
                vsep()

        def argmax8(idx_dst, vals, n, mx_dst):
            b.v(lambda h: h.tensor_reduce(mx_dst, vals, AXX, ALU.max))
            vsep()
            b.v(lambda h, n=n: h.tensor_tensor(gw1[:, 0:n, :], vals,
                                               mx_dst.broadcast_to([P, n, 8]),
                                               ALU.is_equal))
            vsep()
            b.v(lambda h, n=n: h.scalar_tensor_tensor(
                gw1[:, 0:n, :], gw1[:, 0:n, :], -BIG,
                io8B[:].unsqueeze(1).broadcast_to([P, n, 8]), ALU.mult, ALU.add))
            vsep()
            b.v(lambda h: h.tensor_reduce(idx_dst, gw1[:, 0:n, :], AXX, ALU.min))
            vsep()

        def emit_ffn(subs, dma_out=False, xw=None, tap=None):
            n = len(subs)
            ln_stats(subs, 0, xw=xw)
            statv = b.cnt["vector"]
            tprev = [b.cnt["tensor"]]
            vprev = [statv]
            for i, s in enumerate(subs):
                sl = s % 10
                sz = b.s(lambda h, i=i, sl=sl: h.activation(
                    zf[:], x[:, sl, :], AF.Identity, bias=colA[:, i, 0:1],
                    scale=colA[:, i, 1:2]),
                    W(("vector", vprev[0]), ("tensor", tprev[0])))
                vz = b.v(lambda h: h.tensor_copy(zb[:], zf[:]), W(("scalar", sz)))
                vlo = b.v(lambda h: h.tensor_sub(zlo[:], zf[:], zb[:]))
                vprev[0] = vlo
                for c in range(8):
                    b.t(lambda h, c=c: h.transpose(ppT[:, c * P:(c + 1) * P],
                                                   zb[:, c * P:(c + 1) * P], ident[:]),
                        W(("vector", vz)), inc=(c == 7))
                e1 = b.v(lambda h: h.tensor_copy(
                    zhiT[:].rearrange("p c k -> p (c k)"), ppT[:]),
                    W(("tensor", b.cnt["tensor"])))
                for c in range(8):
                    b.t(lambda h, c=c: h.transpose(ppT[:, c * P:(c + 1) * P],
                                                   zlo[:, c * P:(c + 1) * P], ident[:]),
                        W(("vector", e1)), inc=(c == 7))
                e2 = b.v(lambda h: h.tensor_copy(
                    zloT[:].rearrange("p c k -> p (c k)"), ppT[:]),
                    W(("tensor", b.cnt["tensor"])))
                for c in range(8):
                    b.t(lambda h, c=c: h.matmul(pps[:, 0:72], zhiT[:, c, :],
                                                sigc[:, c, :], start=(c == 0),
                                                stop=False),
                        W(("vector", e2)) if c == 0 else (), inc=False)
                    b.t(lambda h, c=c: h.matmul(pp2[:, 0:CH], zhiT[:, c, :],
                                                cw1[:, c, :], start=(c == 0),
                                                stop=False), inc=False)
                for c in range(8):
                    b.t(lambda h, c=c: h.matmul(pps[:, 0:72], zloT[:, c, :],
                                                sigc[:, c, :], start=False,
                                                stop=(c == 7)), inc=(c == 7))
                mm_done = b.cnt["tensor"]
                hbias = b.t(lambda h: h.matmul(pp2[:, 0:CH], ones_row[:], cb1r[:],
                                               start=False, stop=True),
                            W(("sync", ld["ones_row"]), ("sync", ld["cb1"])))
                sv = b.v(lambda h, i=i: h.tensor_copy(scoresA[:, i, :], pps[:, 0:72]),
                         W(("tensor", mm_done)))
                sh = b.s(lambda h: h.activation(hb[:], pp2[:, 0:CH], AF.Gelu),
                         W(("tensor", hbias)))
                for c in range(2):
                    b.t(lambda h, c=c: h.transpose(ppT[:, c * P:(c + 1) * P],
                                                   hb[:, c * P:(c + 1) * P], ident[:]),
                        W(("scalar", sh), ("vector", sv)), inc=(c == 1))
                e3 = b.v(lambda h: h.tensor_copy(
                    hT[:].rearrange("p c k -> p (c k)"), ppT[:, 0:2 * P]),
                    W(("tensor", b.cnt["tensor"])))
                for c in range(2):
                    b.t(lambda h, c=c: h.matmul(pps[:, 256:258], hT[:, c, :],
                                                cw2[:, c, :], start=(c == 0),
                                                stop=(c == 1)),
                        W(("vector", e3)) if c == 0 else (), inc=(c == 1))
                tprev[0] = b.cnt["tensor"]
                b.v(lambda h, i=i: h.tensor_add(compA[:, i, :], pps[:, 256:258],
                                                cb2r[:]), W(("tensor", tprev[0])))
            b.s(lambda h, n=n: h.activation(compA[:, 0:n, :], compA[:, 0:n, :],
                                            AF.Tanh), W(("vector", b.cnt["vector"])))
            tanh_done = b.cnt["scalar"]
            sigma_chain(gw0[:, 0:n, 0:8], scoresA[:, 0:n, 64:72], n)
            calib(gates[:, 0:n, 0:8], gw0[:, 0:n, 0:8], n)
            argmax8(rcol[:, 0:n, 0:1], gates[:, 0:n, 0:8], n, rcol[:, 0:n, 5:6])
            b.v(lambda h, n=n: h.tensor_scalar(spw[:, 0:n, 0:8], spw[:, 0:n, 0:8],
                                               0.0, None, ALU.mult))
            for c in range(8):
                b.v(lambda h, n=n, c=c: h.tensor_scalar(gw3[:, 0:n, 0:1],
                                                        rcol[:, 0:n, 0:1], float(c),
                                                        None, ALU.is_equal))
                vsep()
                b.v(lambda h, n=n, c=c: h.tensor_tensor(
                    gw2[:, 0:n, :], scoresA[:, 0:n, c * 8:(c + 1) * 8],
                    gw3[:, 0:n, 0:1].broadcast_to([P, n, 8]), ALU.mult))
                vsep()
                b.v(lambda h, n=n: h.tensor_add(spw[:, 0:n, 0:8], spw[:, 0:n, 0:8],
                                                gw2[:, 0:n, :]))
                vsep()
            sigma_chain(gw0[:, 0:n, 8:16], spw[:, 0:n, 0:8], n)
            calib(gates[:, 0:n, 8:16], gw0[:, 0:n, 8:16], n)
            argmax8(rcol[:, 0:n, 1:2], gates[:, 0:n, 8:16], n, rcol[:, 0:n, 5:6])
            b.v(lambda h, n=n: h.scalar_tensor_tensor(rcol[:, 0:n, 2:3],
                                                      rcol[:, 0:n, 0:1], 8.0,
                                                      rcol[:, 0:n, 1:2], ALU.mult,
                                                      ALU.add))
            vsep()
            b.v(lambda h, n=n: h.tensor_tensor(
                onehot[:, 0:n, :], io64[:].unsqueeze(1).broadcast_to([P, n, 64]),
                rcol[:, 0:n, 2:3].broadcast_to([P, n, 64]), ALU.is_equal))
            oh_done = b.cnt["vector"]
            for i, s in enumerate(subs):
                tt = b.t(lambda h, i=i: h.transpose(ppT[0:64, 0:P], onehot[:, i, :],
                                                    ident[:]),
                         W(("vector", oh_done)), inc=True)
                ev = b.v(lambda h: h.tensor_copy(ohT[:], ppT[0:64, 0:P]),
                         W(("tensor", tt)))
                pm = b.t(lambda h: h.matmul(pps[:, 0:65], ohT[:], ptab_b[:],
                                            start=True, stop=True),
                         W(("vector", ev)))
                b.v(lambda h, i=i: h.tensor_copy(parsA[:, i, :], pps[:, 0:65]),
                    W(("tensor", pm)))
            b.v(lambda h, n=n: h.tensor_tensor(
                spw[:, 0:n, :], parsA[:, 0:n, 0:16],
                compA[:, 0:n, 0:1].broadcast_to([P, n, 16]), ALU.mult),
                W(("scalar", tanh_done)))
            b.v(lambda h, n=n: h.tensor_tensor(
                spw2[:, 0:n, :], parsA[:, 0:n, 16:32],
                compA[:, 0:n, 1:2].broadcast_to([P, n, 16]), ALU.mult))
            b.v(lambda h, n=n: h.tensor_add(spw[:, 0:n, :], spw[:, 0:n, :],
                                            spw2[:, 0:n, :]))
            b.v(lambda h, n=n: h.tensor_add(spw[:, 0:n, :], spw[:, 0:n, :],
                                            parsA[:, 0:n, 32:48]))
            b.v(lambda h, n=n: h.tensor_relu(spw[:, 0:n, :], spw[:, 0:n, :]))
            b.v(lambda h, n=n: h.tensor_mul(spw[:, 0:n, :], spw[:, 0:n, :],
                                            parsA[:, 0:n, 48:64]))
            b.v(lambda h, n=n: h.tensor_reduce(rcol[:, 0:n, 3:4], spw[:, 0:n, :],
                                               AXX, ALU.add))
            vsep()
            b.v(lambda h, n=n: h.tensor_add(rcol[:, 0:n, 3:4], rcol[:, 0:n, 3:4],
                                            parsA[:, 0:n, 64:65]))
            vsep()
            b.v(lambda h, n=n: h.tensor_scalar(rcol[:, 0:n, 4:5], rcol[:, 0:n, 3:4],
                                               oscr[:], None, ALU.mult),
                W(("sync", ld["osc"])))
            vsep()
            mag_done = b.cnt["vector"]
            if tap == "x1":
                for i in range(n):
                    b.d(lambda h, i=i: h.dma_start(out=dbg["d_gates"][i],
                                                   in_=gates[:, i, :]),
                        W(("vector", mag_done)))
                    b.d(lambda h, i=i: h.dma_start(out=dbg["d_rcol"][i],
                                                   in_=rcol[:, i, :]))
                b.d(lambda h: h.dma_start(out=dbg["d_cols"],
                                          in_=colA[:].rearrange("p s k -> p (s k)")),
                    W(("vector", mag_done)))
                for i in range(n):
                    b.d(lambda h, i=i: h.dma_start(out=dbg["d_scores"][i],
                                                   in_=scoresA[:, i, :]),
                        W(("vector", mag_done)))
                    b.d(lambda h, i=i: h.dma_start(out=dbg["d_sig"][i],
                                                   in_=gw0[:, i, :]))
                    b.d(lambda h, i=i: h.dma_start(out=dbg["d_jsel"][i],
                                                   in_=rcol[:, i, 2:3]))
                    b.d(lambda h, i=i: h.dma_start(out=dbg["d_mag"][i],
                                                   in_=rcol[:, i, 3:4]))
                    b.d(lambda h, i=i: h.dma_start(out=dbg["d_comp"][i],
                                                   in_=compA[:, i, :]))
                    b.d(lambda h, i=i: h.dma_start(out=dbg["d_pars"][i],
                                                   in_=parsA[:, i, :]))
            yprev = [mag_done]
            for i, s in enumerate(subs):
                sl = s % 10
                tt = b.t(lambda h, i=i: h.transpose(ppT[0:64, 0:P], onehot[:, i, :],
                                                    ident[:]),
                         W(("vector", mag_done)), inc=True)
                ev = b.v(lambda h: h.tensor_copy(ohT[:], ppT[0:64, 0:P]),
                         W(("tensor", tt)))
                for n2 in range(2):
                    b.t(lambda h, n2=n2: h.matmul(pp0[:, n2 * 512:(n2 + 1) * 512],
                                                  ohT[:],
                                                  dirs_b[:, n2 * 512:(n2 + 1) * 512],
                                                  start=True, stop=True),
                        W(("vector", ev)) if n2 == 0 else (), inc=(n2 == 1))
                dmm = b.cnt["tensor"]
                sz = b.s(lambda h, i=i, sl=sl: h.activation(
                    zf[:], x[:, sl, :], AF.Identity, bias=colA[:, i, 0:1],
                    scale=colA[:, i, 1:2]), W(("vector", yprev[0])))
                yv = b.v(lambda h, i=i: h.scalar_tensor_tensor(
                    ybuf[:], pp0[:], rcol[:, i, 4:5], zf[:], ALU.mult, ALU.add),
                    W(("tensor", dmm), ("scalar", sz)))
                yprev[0] = yv
                b.v(lambda h: h.bn_stats(statsA[:, 0, 0:6], ybuf[:, 0:512]))
                b.v(lambda h: h.bn_stats(statsA[:, 0, 6:12], ybuf[:, 512:D]))
                vsep()
                vsep()
                st1 = statsA[:, 0, :].rearrange("p (t e) -> p t e", t=4)
                b.v(lambda h: h.tensor_reduce(colY[:, 0:1], st1[:, :, 1], AXX,
                                              ALU.add))
                b.v(lambda h: h.tensor_mul(colY[:, 4:8], st1[:, :, 1], st1[:, :, 1]))
                vsep()
                b.v(lambda h: h.tensor_reduce(colY[:, 1:2], colY[:, 4:8], AXX,
                                              ALU.add))
                b.v(lambda h: h.tensor_reduce(colY[:, 2:3], st1[:, :, 2], AXX,
                                              ALU.add))
                b.v(lambda h: h.tensor_scalar(colY[:, 0:1], colY[:, 0:1], 0.25, None,
                                              ALU.mult))
                vsep()
                b.v(lambda h: h.tensor_scalar(colY[:, 1:2], colY[:, 1:2], 0.25, None,
                                              ALU.mult))
                vsep()
                b.v(lambda h: h.scalar_tensor_tensor(colY[:, 1:2], colY[:, 2:3],
                                                     1.0 / D, colY[:, 1:2], ALU.mult,
                                                     ALU.add))
                b.v(lambda h: h.tensor_mul(colY[:, 2:3], colY[:, 0:1], colY[:, 0:1]))
                vsep()
                b.v(lambda h: h.tensor_sub(colY[:, 1:2], colY[:, 1:2], colY[:, 2:3]))
                vsep()
                b.v(lambda h: h.tensor_scalar(colY[:, 1:2], colY[:, 1:2], EPS, None,
                                              ALU.add))
                vsep()
                rsqrt_newton(colY[:, 3:4], colY[:, 1:2], 1)
                vsep()
                b.v(lambda h: h.tensor_mul(colY[:, 2:3], colY[:, 0:1], colY[:, 3:4]))
                vsep()
                vr = b.v(lambda h: h.tensor_scalar(colY[:, 2:3], colY[:, 2:3],
                                                   -1.0, None, ALU.mult))
                sl2 = b.s(lambda h: h.activation(zf[:], ybuf[:], AF.Identity,
                                                 bias=colY[:, 2:3],
                                                 scale=colY[:, 3:4]),
                          W(("vector", vr)))
                b.v(lambda h: h.tensor_mul(zf[:], zf[:], g05[:]),
                    W(("scalar", sl2)))
                b.v(lambda h: h.tensor_add(zf[:], zf[:], bt05[:]))
                xu = b.v(lambda h, sl=sl: h.tensor_add(x[:, sl, :], x[:, sl, :],
                                                       zf[:]))
                state["x2v"][s] = xu
                yprev[0] = xu
                if dma_out:
                    b.d(lambda h, s=s, sl=sl: h.dma_start(out=xout[s - 2],
                                                          in_=x[:, sl, :]),
                        W(("vector", xu)))
                if tap == "x1":
                    b.d(lambda h, i=i, sl=sl: h.dma_start(out=dbg["d_x1"][i],
                                                          in_=x[:, sl, :]),
                        W(("vector", xu)))
                if tap == "x2o":
                    b.d(lambda h, i=i, sl=sl: h.dma_start(out=dbg["d_x2"][i],
                                                          in_=x[:, sl, :]),
                        W(("vector", xu)))

        def conv_window(s, ring, vwait, kt, out_cb):
            for mt, mats, pdst in ((0, fwd_re, pp0), (1, fwd_imp, pp1)):
                for kc, wslot in ((0, (s - 1) % 2), (1, s % 2)):
                    for n2 in range(2):
                        b.t(lambda h, mats=mats, kc=kc, n2=n2, pdst=pdst, wslot=wslot:
                            h.matmul(pdst[:, n2 * 512:(n2 + 1) * 512], mats[:, kc, :],
                                     ring[:, wslot, n2 * 512:(n2 + 1) * 512],
                                     start=(kc == 0), stop=(kc == 1)),
                            W(("vector", vwait)) if (mt == 0 and kc == 0 and n2 == 0)
                            else (), inc=(mt == 1 and kc == 1 and n2 == 1))
            fwd_done = b.cnt["tensor"]
            b.v(lambda h, kt=kt: h.tensor_tensor(fq1[:], pp0[:], kt["A1"][:],
                                                 ALU.mult), W(("tensor", fwd_done)))
            b.v(lambda h, kt=kt: h.tensor_tensor(fqt[:], pp1[:], kt["B"][:], ALU.mult))
            b.v(lambda h: h.tensor_sub(fq1[:], fq1[:], fqt[:]))
            b.v(lambda h, kt=kt: h.tensor_tensor(fq2[:], pp0[:], kt["B"][:], ALU.mult))
            b.v(lambda h, kt=kt: h.tensor_tensor(fqt[:], pp1[:], kt["A2"][:],
                                                 ALU.mult))
            fm = b.v(lambda h: h.tensor_add(fq2[:], fq2[:], fqt[:]))
            for n2 in range(2):
                b.t(lambda h, n2=n2: h.matmul(pp2[:, n2 * 512:(n2 + 1) * 512],
                                              inv_c[:], fq1[:, n2 * 512:(n2 + 1) * 512],
                                              start=True, stop=False),
                    W(("vector", fm)) if n2 == 0 else (), inc=False)
            for n2 in range(2):
                b.t(lambda h, n2=n2: h.matmul(pp2[:, n2 * 512:(n2 + 1) * 512],
                                              inv_s[:], fq2[:, n2 * 512:(n2 + 1) * 512],
                                              start=False, stop=True), inc=(n2 == 1))
            return b.v(out_cb, W(("tensor", b.cnt["tensor"])))

        def emit_attn(subs, tap=None):
            n = len(subs)
            ln_stats(subs, 8)
            b.v(lambda h, n=n: h.tensor_scalar(colA[:, 0:n, 11], colA[:, 0:n, 11],
                                               -EPS, None, ALU.add))
            b.v(lambda h, n=n: h.tensor_mul(colA[:, 0:n, 12], colA[:, 0:n, 9],
                                            colA[:, 0:n, 9]))
            vsep()
            b.v(lambda h, n=n: h.tensor_mul(colA[:, 0:n, 12], colA[:, 0:n, 12],
                                            colA[:, 0:n, 11]))
            vsep()
            b.v(lambda h, n=n: h.tensor_scalar(colA[:, 0:n, 12], colA[:, 0:n, 12],
                                               EPS, None, ALU.add))
            vsep()
            rsqrt_newton(colA[:, 0:n, 13], colA[:, 0:n, 12], n)
            vsep()
            b.v(lambda h, n=n: h.tensor_mul(colA[:, 0:n, 10], colA[:, 0:n, 9],
                                            colA[:, 0:n, 13]))
            b.v(lambda h, n=n: h.tensor_mul(colA[:, 0:n, 12], colA[:, 0:n, 8],
                                            colA[:, 0:n, 13]))
            b.v(lambda h, n=n: h.tensor_scalar(colA[:, 0:n, 11], colA[:, 0:n, 9],
                                               1.0, None, ALU.add))
            stat_done = b.cnt["vector"]
            tprev = [b.cnt["tensor"]]
            for i, s in enumerate(subs):
                sl = s % 10
                sz = b.s(lambda h, i=i, sl=sl: h.activation(
                    zb[:], x[:, sl, :], AF.Identity, bias=colA[:, i, 12:13],
                    scale=colA[:, i, 10:11]),
                    W(("vector", stat_done), ("tensor", tprev[0])))
                for c in range(8):
                    b.t(lambda h, c=c: h.transpose(ppT[:, c * P:(c + 1) * P],
                                                   zb[:, c * P:(c + 1) * P],
                                                   ident[:]),
                        W(("scalar", sz)), inc=(c == 7))
                e1 = b.v(lambda h: h.tensor_copy(
                    zhiT[:].rearrange("p c k -> p (c k)"), ppT[:]),
                    W(("tensor", b.cnt["tensor"])))
                for c in range(8):
                    for n6 in range(6):
                        pdst = (pp0, pp1, pp2)[n6 // 2]
                        off = (n6 % 2) * 512
                        b.t(lambda h, c=c, n6=n6, pdst=pdst, off=off: h.matmul(
                            pdst[:, off:off + 512], zhiT[:, c, :],
                            winp[:, c, n6 * 512:(n6 + 1) * 512],
                            start=(c == 0), stop=(c == 7)),
                            W(("vector", e1)) if (c == 0 and n6 == 0) else (),
                            inc=(c == 7 and n6 == 5))
                proj_done = b.cnt["tensor"]
                b.v(lambda h, s=s: h.tensor_add(vring[:, s % 2, :], pp0[:],
                                                biasr[:, 0:D]),
                    W(("tensor", proj_done)))
                if s < 2:
                    b.v(lambda h, s=s: h.tensor_scalar(vring[:, s % 2, :],
                                                       vring[:, s % 2, :],
                                                       maskc[:, s:s + 1], None,
                                                       ALU.mult),
                        W(("sync", ld["mask"])))
                b.v(lambda h: h.tensor_add(g1c[:], pp1[:], biasr[:, D:2 * D]))
                b.v(lambda h: h.tensor_add(g2c[:], pp2[:], biasr[:, 2 * D:3 * D]))
                state["vready"][s] = b.cnt["vector"]
                if tap and s == 2:
                    b.d(lambda h, s=s: h.dma_start(out=dbg["d_v"][0],
                                                   in_=vring[:, s % 2, :]),
                        W(("vector", state["vready"][s])))
                    b.d(lambda h: h.dma_start(out=dbg["d_g1"][0], in_=g1c[:]))
                if s >= 1:
                    uv = conv_window(
                        s, vring, max(state["vready"][s],
                                      state["vready"].get(s - 1, 0)), ktab[0],
                        lambda h, s=s: h.tensor_tensor(uring[:, s % 2, :], pp2[:],
                                                       g1c[:], ALU.mult))
                    state["uready"][s] = uv
                    if tap and s == 2:
                        b.d(lambda h, s=s: h.dma_start(out=dbg["d_u"][0],
                                                       in_=uring[:, s % 2, :]),
                            W(("vector", uv)))
                if s >= 2:
                    yv = conv_window(
                        s, uring, max(state["uready"][s],
                                      state["uready"].get(s - 1, 0)), ktab[1],
                        lambda h: h.tensor_tensor(ygc[:], pp2[:], g2c[:], ALU.mult))
                    if tap and s == 2:
                        b.d(lambda h: h.dma_start(out=dbg["d_yg"][0], in_=ygc[:]),
                            W(("vector", yv)))
                    for c in range(8):
                        b.t(lambda h, c=c: h.transpose(ppT[:, c * P:(c + 1) * P],
                                                       ygc[:, c * P:(c + 1) * P],
                                                       ident[:]),
                            W(("vector", yv)), inc=(c == 7))
                    e2 = b.v(lambda h: h.tensor_copy(
                        zloT[:].rearrange("p c k -> p (c k)"), ppT[:]),
                        W(("tensor", b.cnt["tensor"])))
                    for c in range(8):
                        for n2 in range(2):
                            b.t(lambda h, c=c, n2=n2: h.matmul(
                                pp0[:, n2 * 512:(n2 + 1) * 512], zloT[:, c, :],
                                wout[:, c, n2 * 512:(n2 + 1) * 512],
                                start=(c == 0), stop=(c == 7)),
                                W(("vector", e2)) if (c == 0 and n2 == 0) else (),
                                inc=(c == 7 and n2 == 1))
                    wo_done = b.cnt["tensor"]
                    b.v(lambda h: h.tensor_add(ybuf[:], pp0[:], boutr[:]),
                        W(("tensor", wo_done)))
                    b.v(lambda h, i=i: h.tensor_scalar(ybuf[:], ybuf[:],
                                                       colA[:, i, 8:9], None,
                                                       ALU.add))
                    xu = b.v(lambda h, i=i, sl=sl: h.scalar_tensor_tensor(
                        x[:, sl, :], x[:, sl, :], colA[:, i, 11:12], ybuf[:],
                        ALU.mult, ALU.add))
                    state["x2v"][s] = xu
                tprev[0] = b.cnt["tensor"]

        emit_ffn(HALF1, xw=x_loaded, tap=("x1" if debug else None))
        emit_attn(HALF1, tap=debug)
        emit_ffn(HALF1[2:], dma_out=True, tap=("x2o" if debug else None))
        x_loaded2 = {}
        reload_wait = b.cnt["sync"]
        for s in HALF2:
            x_loaded2[s] = b.d(
                lambda h, s=s: h.dma_start(out=x[:, s % 10, :], in_=di["xin"][s]),
                W(("vector", b.cnt["vector"]), ("sync", reload_wait)))
        emit_ffn(HALF2, xw=x_loaded2)
        emit_attn(HALF2)
        emit_ffn(HALF2, dma_out=True)

        b.g(lambda h: None, W(("sync", b.cnt["sync"])))
        b.run(block, sems)

    return nc



def _in_maps(inputs):
    f32 = np.float32
    bf16 = ml_dtypes.bfloat16
    x = np.ascontiguousarray(np.asarray(inputs["L0_attn_x"]), dtype=f32)
    W_in = np.ascontiguousarray(np.asarray(inputs["L0_attn_W_in"], f32).reshape(8, P, D3))
    W_out = np.ascontiguousarray(np.asarray(inputs["L0_attn_W_out"], f32).reshape(8, P, D))
    Cw1 = np.ascontiguousarray(np.asarray(inputs["L0_ffn_compress_w1"], f32).reshape(8, P, CH))
    Cw2 = np.ascontiguousarray(np.asarray(inputs["L0_ffn_compress_w2"], f32).reshape(2, P, 2))
    k0c = np.ascontiguousarray(np.asarray(inputs["L0_attn_k0"], f32)[0].reshape(8, P, P))
    k1c = np.ascontiguousarray(np.asarray(inputs["L0_attn_k1"], f32)[0].reshape(8, P, P))
    gamma = np.asarray(inputs["L0_attn_gamma"], f32)
    beta = np.asarray(inputs["L0_attn_beta"], f32)
    gamma_cols = np.ascontiguousarray(gamma.reshape(8, P).T)
    beta_rep = np.ascontiguousarray(
        np.broadcast_to(beta.reshape(8, P, 1), (8, P, P)).transpose(1, 0, 2)
    ).astype(bf16)
    b_in_rep = np.ascontiguousarray(
        np.broadcast_to(np.asarray(inputs["L0_attn_b_in"], f32)[None, :], (P, D3)))
    b_out_rep = np.ascontiguousarray(
        np.broadcast_to(np.asarray(inputs["L0_attn_b_out"], f32)[None, :], (P, D)))
    cb1_row = np.ascontiguousarray(
        np.asarray(inputs["L0_ffn_compress_b1"], f32)[None, :]).astype(bf16)
    cb2_rep = np.ascontiguousarray(
        np.broadcast_to(np.asarray(inputs["L0_ffn_compress_b2"], f32)[None, :], (P, 2)))
    gf_rep = np.ascontiguousarray(
        np.broadcast_to(np.asarray(inputs["L0_ffn_norm_gamma"], f32)[None, :], (P, D)))
    bf_rep = np.ascontiguousarray(
        np.broadcast_to(np.asarray(inputs["L0_ffn_norm_beta"], f32)[None, :], (P, D)))
    dirs = np.ascontiguousarray(np.asarray(inputs["L0_ffn_directions"], f32))
    ptab = np.ascontiguousarray(np.concatenate([
        np.asarray(inputs["L0_ffn_spline_w1"], f32).reshape(64, 32),
        np.asarray(inputs["L0_ffn_spline_b1"], f32),
        np.asarray(inputs["L0_ffn_spline_w2"], f32).reshape(64, 16),
        np.asarray(inputs["L0_ffn_spline_b2"], f32).reshape(64, 1)], axis=1))
    knots_rep = np.ascontiguousarray(
        np.broadcast_to(np.asarray(inputs["L0_ffn_knot_values"], f32)[None, :], (P, 8)))
    temp_rep = np.ascontiguousarray(
        np.broadcast_to(np.asarray(inputs["L0_ffn_temperature"], f32)[None, :], (P, 1)))
    osc_rep = np.ascontiguousarray(
        np.broadcast_to(np.asarray(inputs["L0_ffn_output_scale"], f32)[None, :], (P, 1)))
    fwd_re, fwd_imp, inv_c, inv_s = _dft_tables()
    ident = np.eye(P, dtype=bf16)
    io64 = np.ascontiguousarray(
        np.broadcast_to(np.arange(64, dtype=f32)[None, :], (P, 64)))
    io8B = np.ascontiguousarray(
        np.broadcast_to((np.arange(8).astype(np.float64) + BIG).astype(f32)[None, :],
                        (P, 8)))
    ones_row = np.ones((1, P), bf16)

    shared = dict(W_in=W_in, W_out=W_out, Cw1=Cw1, Cw2=Cw2,
                  sigs_raw=np.ascontiguousarray(np.asarray(inputs["L0_ffn_signatures_raw"], f32)),
                  k0c=k0c, k1c=k1c, gamma_cols=gamma_cols, beta_rep=beta_rep,
                  b_in_rep=b_in_rep, b_out_rep=b_out_rep, cb1_row=cb1_row,
                  cb2_rep=cb2_rep, gf_rep=gf_rep, bf_rep=bf_rep, dirs=dirs,
                  ptab=ptab, knots_rep=knots_rep, temp_rep=temp_rep,
                  osc_rep=osc_rep, fwd_re=fwd_re, fwd_imp=fwd_imp, inv_c=inv_c,
                  inv_s=inv_s, ident=ident, io64=io64, io8B=io8B,
                  ones_row=ones_row)

    maps = []
    for core in range(8):
        bseq, half = core // 2, core % 2
        start = half * 2048
        xin = np.zeros((S_TOT, P, D), f32)
        if half == 1:
            xin[:] = x[bseq, start - 256:start + 2048].reshape(S_TOT, P, D)
        else:
            xin[2:] = x[bseq, start:start + 2048].reshape(16, P, D)
        m = dict(shared)
        m["xin"] = xin
        m["halo_mask"] = np.full((P, 2), 1.0 if half == 1 else 0.0, f32)
        maps.append(m)
    return maps


def kernel(**inputs):
    maps = _in_maps(inputs)
    nc = build_nc(debug=_DEBUG)
    res = run_bass_kernel_spmd(nc, maps, core_ids=list(range(8)))
    out = np.empty((4, 4096, 1024), np.float32)
    for core in range(8):
        bseq, half = core // 2, core % 2
        xo = res.results[core]["xout"].reshape(2048, D)
        out[bseq, half * 2048:(half + 1) * 2048] = xo
    return out


# version 02
# speedup vs baseline: 1.0375x; 1.0375x over previous
import contextlib

import numpy as np
import ml_dtypes

import concourse.bass as bass
import concourse.mybir as mybir
from concourse.bass_utils import run_bass_kernel_spmd

F32 = mybir.dt.float32
BF16 = mybir.dt.bfloat16
I32 = mybir.dt.int32
AF = mybir.ActivationFunctionType
ALU = mybir.AluOpType
AXX = mybir.AxisListType.X

P = 128
D = 1024
D3 = 3072
CH = 256
S_TOT = 18
HALF1 = list(range(0, 10))
HALF2 = list(range(10, 18))
EPS = 1e-5
BIG = 1024.0

_DEBUG = False


def _dft_tables():
    n = 256
    j = np.arange(n)[:, None].astype(np.float64)
    k = np.arange(128)[None, :].astype(np.float64)
    ang = 2.0 * np.pi * (j * k) / n
    fwd_re = np.cos(ang)
    fwd_imp = -np.sin(ang)
    fwd_imp[:, 0] = np.cos(np.pi * j[:, 0])
    tau = np.arange(128)[None, :].astype(np.float64)
    jj = 128.0 + tau
    kk = np.arange(128)[:, None].astype(np.float64)
    c = np.cos(2.0 * np.pi * kk * jj / n) * (2.0 / n)
    c[0, :] = 1.0 / n
    s = -np.sin(2.0 * np.pi * kk * jj / n) * (2.0 / n)
    s[0, :] = np.cos(np.pi * jj[0]) * (1.0 / n)
    bf = ml_dtypes.bfloat16
    return (np.ascontiguousarray(fwd_re.reshape(2, 128, 128).transpose(1, 0, 2)).astype(bf),
            np.ascontiguousarray(fwd_imp.reshape(2, 128, 128).transpose(1, 0, 2)).astype(bf),
            c.astype(bf), s.astype(bf))


class B:
    ENGS = ("tensor", "vector", "scalar", "gpsimd", "sync")

    def __init__(self):
        self.streams = {e: [] for e in self.ENGS}
        self.cnt = {e: 0 for e in self.ENGS}

    def op(self, eng, fn, waits=(), inc=True):
        step = 16 if eng == "sync" else 1
        if inc:
            self.cnt[eng] += step
        self.streams[eng].append((fn, tuple(waits), inc, self.cnt[eng]))
        return self.cnt[eng]

    def t(self, fn, waits=(), inc=True):
        return self.op("tensor", fn, waits, inc)

    def v(self, fn, waits=(), inc=True):
        return self.op("vector", fn, waits, inc)

    def s(self, fn, waits=(), inc=True):
        return self.op("scalar", fn, waits, inc)

    def g(self, fn, waits=(), inc=True):
        return self.op("gpsimd", fn, waits, inc)

    def d(self, fn, waits=(), inc=True):
        return self.op("sync", fn, waits, inc)

    def run(self, block, sems):
        def make_body(eng_name):
            stream = self.streams[eng_name]
            step = 16 if eng_name == "sync" else 1

            def body(h):
                waited = {}
                for fn, waits, inc, val in stream:
                    for we, wv in waits:
                        if waited.get(we, -1) < wv:
                            h.wait_ge(sems[we], wv)
                            waited[we] = wv
                    inst = fn(h)
                    if inc:
                        if inst is None:
                            inst = h.nop()
                        inst.then_inc(sems[eng_name], step)
            return body

        block.tensor(make_body("tensor"))
        block.vector(make_body("vector"))
        block.scalar(make_body("scalar"))
        block.gpsimd(make_body("gpsimd"))
        block.sync(make_body("sync"))


def build_nc(debug=False):
    nc = bass.Bass("TRN2", target_bir_lowering=False)
    di = {}

    def din(name, shape, dt=F32):
        di[name] = nc.dram_tensor(name, shape, dt, kind="ExternalInput").ap()
        return di[name]

    din("xin", [S_TOT, P, D])
    din("halo_mask", [P, 2])
    din("W_in", [8, P, D3])
    din("W_out", [8, P, D])
    din("Cw1", [8, P, CH])
    din("Cw2", [2, P, 2])
    din("sigs_raw", [64, D])
    din("k0c", [8, P, P])
    din("k1c", [8, P, P])
    din("gamma_cols", [P, 8])
    din("beta_rep", [P, 8, P], BF16)
    din("b_in_rep", [P, D3])
    din("b_out_rep", [P, D])
    din("cb1_row", [1, CH], BF16)
    din("cb2_rep", [P, 2])
    din("gf_rep", [P, D])
    din("bf_rep", [P, D])
    din("dirs", [64, D])
    din("ptab", [64, 65])
    din("knots_rep", [P, 8])
    din("temp_rep", [P, 1])
    din("osc_rep", [P, 1])
    din("fwd_re", [P, 2, P], BF16)
    din("fwd_imp", [P, 2, P], BF16)
    din("inv_c", [P, P], BF16)
    din("inv_s", [P, P], BF16)
    din("ident", [P, P], BF16)
    din("io64", [P, 64])
    din("io8B", [P, 8])
    din("ones_row", [1, P], BF16)

    xout = nc.dram_tensor("xout", [16, P, D], F32, kind="ExternalOutput").ap()

    dbg = {}
    if debug:
        for nm, shape, dt_ in [("d_scores", [10, P, 72], F32), ("d_sig", [10, P, 16], F32),
                               ("d_jsel", [10, P, 1], F32), ("d_mag", [10, P, 1], F32),
                               ("d_x1", [10, P, D], F32), ("d_v", [3, P, D], BF16),
                               ("d_g1", [3, P, D], BF16), ("d_u", [3, P, D], BF16),
                               ("d_yg", [3, P, D], BF16), ("d_x2", [10, P, D], F32),
                               ("d_comp", [10, P, 2], F32), ("d_pars", [10, P, 65], F32),
                               ("d_cols", [P, 160], F32), ("d_gates", [10, P, 16], F32),
                               ("d_rcol", [10, P, 8], F32), ("d_ybuf", [P, D], F32),
                               ("d_lnb", [P, D], BF16), ("d_tb", [P, D], BF16),
                               ("d_colY", [P, 8], F32)]:
            dbg[nm] = nc.dram_tensor(nm, shape, dt_, kind="ExternalOutput").ap()

    b = B()

    with contextlib.ExitStack() as ctx:
        def sb(name, shape, dt):
            return ctx.enter_context(nc.sbuf_tensor(name, shape, dt))

        x = sb("x", [P, 10, D], F32)
        winp = sb("winp", [P, 8, D3], BF16)
        wout = sb("wout", [P, 8, D], BF16)
        cw1 = sb("cw1", [P, 8, CH], BF16)
        cw2 = sb("cw2", [P, 2, 2], BF16)
        sigc = sb("sigc", [P, 8, 72], BF16)
        biasr = sb("biasr", [P, D3], BF16)
        cb1r = sb("cb1r", [1, CH], BF16)
        cb2r = sb("cb2r", [P, 2], F32)
        boutr = sb("boutr", [P, D], F32)
        g05 = sb("g05", [P, D], F32)
        bt05 = sb("bt05", [P, D], F32)
        dirs_b = sb("dirs_b", [64, D], BF16)
        ptab_b = sb("ptab_b", [64, 65], BF16)
        ktab = {kk: dict(A1=sb(f"A1_{kk}", [P, D], BF16),
                         B=sb(f"B_{kk}", [P, D], BF16),
                         A2=sb(f"A2_{kk}", [P, D], BF16)) for kk in (0, 1)}
        fwd_re = sb("fwd_re_s", [P, 2, P], BF16)
        fwd_imp = sb("fwd_imp_s", [P, 2, P], BF16)
        inv_c = sb("inv_c_s", [P, P], BF16)
        inv_s = sb("inv_s_s", [P, P], BF16)
        ident = sb("ident_s", [P, P], BF16)
        io64 = sb("io64_s", [P, 64], F32)
        io8B = sb("io8B_s", [P, 8], F32)
        ones_row = sb("ones_row_s", [1, P], BF16)
        knots = sb("knots_s", [P, 8], F32)
        dknot = sb("dknot_s", [P, 7], F32)
        invt = sb("invt_s", [P, 1], F32)
        oscr = sb("oscr_s", [P, 1], F32)
        maskc = sb("maskc", [P, 2], F32)

        stg = sb("stg", [P, D3 // 2], F32)
        zf = sb("zf", [P, D], F32)
        zb = sb("zb", [P, D], BF16)
        zlo = sb("zlo", [P, D], BF16)
        zhiT = sb("zhiT", [P, 8, P], BF16)
        zloT = sb("zloT", [P, 8, P], BF16)
        hb = sb("hb", [P, CH], BF16)
        hT = sb("hT", [P, 2, P], BF16)
        ohT = sb("ohT", [64, P], BF16)

        NS = 10
        scoresA = sb("scoresA", [P, NS, 72], F32)
        statsA = sb("statsA", [P, NS, 12], F32)
        colA = sb("colA", [P, NS, 16], F32)
        gw0 = sb("gw0", [P, NS, 16], F32)
        gw1 = sb("gw1", [P, NS, 8], F32)
        gw2 = sb("gw2", [P, NS, 8], F32)
        gw3 = sb("gw3", [P, NS, 8], F32)
        gates = sb("gates", [P, NS, 16], F32)
        rcol = sb("rcol", [P, NS, 8], F32)
        onehot = sb("onehot", [P, NS, 64], BF16)
        compA = sb("compA", [P, NS, 2], F32)
        spw = sb("spw", [P, NS, 16], F32)
        spw2 = sb("spw2", [P, NS, 16], F32)
        parsA = sb("parsA", [P, NS, 65], F32)
        ybuf = sb("ybuf", [P, D], F32)
        ybuf2 = sb("ybuf2", [P, D], F32)

        vring = sb("vring", [P, 2, D], BF16)
        uring = sb("uring", [P, 2, D], BF16)
        g1c = sb("g1c", [P, D], BF16)
        g2c = sb("g2c", [P, D], BF16)
        ygc = sb("ygc", [P, D], BF16)
        fq1 = sb("fq1", [P, D], BF16)
        fq2 = sb("fq2", [P, D], BF16)
        fqt = sb("fqt", [P, D], BF16)
        fxr = parsA[:].rearrange("p s k -> p (s k)")[:, 0:512].bitcast(BF16)
        fxi = scoresA[:].rearrange("p s k -> p (s k)")[:, 0:512].bitcast(BF16)

        pp0 = ctx.enter_context(nc.psum_tensor("pp0", [P, D], F32))
        pp1 = ctx.enter_context(nc.psum_tensor("pp1", [P, D], F32))
        pp2 = ctx.enter_context(nc.psum_tensor("pp2", [P, D], F32))
        ppT = ctx.enter_context(nc.psum_tensor("ppT", [P, D], BF16))
        pps = ctx.enter_context(nc.psum_tensor("pps", [P, 512], F32))

        sems = {e: ctx.enter_context(nc.semaphore(f"sem_{e}")) for e in B.ENGS}
        block = ctx.enter_context(nc.Block())

        def W(*pairs):
            return tuple(pairs)

        def vsep():
            b.v(lambda h: h.tensor_scalar(sepA[:], sepA[:], 0.0, None, ALU.mult),
                inc=False)

        def ssep():
            b.s(lambda h: h.activation(sepA[:], sepA[:], AF.Identity), inc=False)


        ld = {}
        stg_free = [0]

        def load(name, out_ap, in_ap, waits=()):
            ld[name] = b.d(lambda h, o=out_ap, i=in_ap: h.dma_start(out=o, in_=i),
                           waits)
            return ld[name]

        def load_stg(name, in_ap, cols, rows=P):
            return load(name, stg[:rows, 0:cols], in_ap, W(("vector", stg_free[0])))

        load("ident", ident[:], di["ident"])
        load("io64", io64[:], di["io64"])
        load("io8B", io8B[:], di["io8B"])
        load("ones_row", ones_row[:], di["ones_row"])
        load("knots", knots[:], di["knots_rep"])
        load("temp", invt[:], di["temp_rep"])
        load("osc", oscr[:], di["osc_rep"])
        load("mask", maskc[:], di["halo_mask"])
        load("fwd_re", fwd_re[:], di["fwd_re"])
        load("fwd_imp", fwd_imp[:], di["fwd_imp"])
        load("inv_c", inv_c[:], di["inv_c"])
        load("inv_s", inv_s[:], di["inv_s"])
        load("cb1", cb1r[:], di["cb1_row"])
        load("cb2", cb2r[:], di["cb2_rep"])
        load("bout", boutr[:], di["b_out_rep"])
        load("beta", zhiT[:], di["beta_rep"])
        x_loaded = {}
        for s in range(10):
            x_loaded[s] = load(f"x{s}", x[:, s, :], di["xin"][s])

        sigs16 = ygc
        load_stg("sigs_raw", di["sigs_raw"], D, rows=64)
        b.v(lambda h: h.tensor_scalar(zf[:64, :], stg[:64, 0:D], 0.3, None, ALU.is_gt),
            W(("sync", ld["sigs_raw"])))
        b.v(lambda h: h.tensor_scalar(ybuf[:64, :], stg[:64, 0:D], -0.3, None, ALU.is_lt))
        b.v(lambda h: h.tensor_sub(zf[:64, :], zf[:64, :], ybuf[:64, :]))
        sig_cast = b.v(lambda h: h.tensor_copy(sigs16[:64, :], zf[:64, :]))
        stg_free[0] = sig_cast
        for c in range(8):
            b.t(lambda h, c=c: h.transpose(ppT[:, c * 64:(c + 1) * 64],
                                           sigs16[:64, c * P:(c + 1) * P],
                                           ident[:64, :64]),
                W(("vector", sig_cast), ("sync", ld["ident"])), inc=(c == 7))
        tp = b.cnt["tensor"]
        for c in range(8):
            b.v(lambda h, c=c: h.tensor_copy(sigc[:, c, 0:64], ppT[:, c * 64:(c + 1) * 64]),
                W(("tensor", tp)))
        cs_v = sb("cs_v", [P, 8, 8], F32)
        cs_w = sb("cs_w", [P, 8, 8], F32)
        b.v(lambda h: h.tensor_reduce(
            cs_v[:],
            sigc[:, :, 0:64].rearrange("p c (g l) -> p c g l", g=8), AXX, ALU.add))
        vsep()
        b.v(lambda h: h.tensor_scalar(cs_w[:], cs_v[:], 0.0, None, ALU.is_gt))
        vsep()
        b.v(lambda h: h.tensor_scalar(cs_v[:], cs_v[:], 0.0, None, ALU.is_lt))
        vsep()
        b.v(lambda h: h.tensor_sub(cs_w[:], cs_w[:], cs_v[:]))
        for c in range(8):
            b.v(lambda h, c=c: h.tensor_copy(sigc[:, c, 64:72], cs_w[:, c, :]))

        def rsqrt_newton(dst, src, k, waits=()):
            b.v(lambda h: h.tensor_copy(rsA[:, 0:k], src), waits)
            vsep()
            vsep()
            b.v(lambda h: h.tensor_scalar(rsB[:].bitcast(I32), rsA[:].bitcast(I32), 1,
                                          None, ALU.logical_shift_right))
            vsep()
            b.v(lambda h: h.tensor_scalar(rsB[:].bitcast(I32), rsB[:].bitcast(I32), -1,
                                          0x5F3759DF, ALU.mult, ALU.add))
            vsep()
            for _ in range(3):
                b.v(lambda h: h.tensor_mul(rsC[:], rsB[:], rsB[:]))
                vsep()
                b.v(lambda h: h.tensor_mul(rsC[:], rsC[:], rsA[:]))
                vsep()
                b.v(lambda h: h.tensor_scalar(rsC[:], rsC[:], -0.5, 1.5, ALU.mult,
                                              ALU.add))
                vsep()
                b.v(lambda h: h.tensor_mul(rsB[:], rsB[:], rsC[:]))
                vsep()
            b.v(lambda h: h.tensor_copy(dst, rsB[:, 0:k]))
            vsep()

        tc1 = sb("tc1", [P, 4], F32)
        rsA = sb("rsA", [P, 64], F32)
        rsB = sb("rsB", [P, 64], F32)
        rsC = sb("rsC", [P, 64], F32)
        sepA = sb("sepA", [P, 64], F32)
        colY = sb("colY", [P, 24], F32)
        b.v(lambda h: h.tensor_scalar(tc1[:, 0:1], invt[:], 1e-6, None, ALU.add),
            W(("sync", ld["temp"])))
        vsep()
        rsqrt_newton(tc1[:, 1:2], tc1[:, 0:1], 1)
        vsep()
        b.v(lambda h: h.tensor_mul(invt[:], tc1[:, 1:2], tc1[:, 1:2]))
        b.v(lambda h: h.tensor_sub(dknot[:], knots[:, 1:8], knots[:, 0:7]),
            W(("sync", ld["knots"])))

        load_stg("gf", di["gf_rep"], D)
        stg_free[0] = b.v(lambda h: h.tensor_scalar(g05[:], stg[:, 0:D], 0.5, None,
                                                    ALU.mult), W(("sync", ld["gf"])))
        load_stg("bf", di["bf_rep"], D)
        stg_free[0] = b.v(lambda h: h.tensor_scalar(bt05[:], stg[:, 0:D], 0.5, None,
                                                    ALU.mult), W(("sync", ld["bf"])))
        load_stg("dirs_f", di["dirs"], D, rows=64)
        stg_free[0] = b.v(lambda h: h.tensor_copy(dirs_b[:], stg[:64, 0:D]),
                          W(("sync", ld["dirs_f"])))
        load_stg("ptab_f", di["ptab"], 65, rows=64)
        stg_free[0] = b.v(lambda h: h.tensor_copy(ptab_b[:], stg[:64, 0:65]),
                          W(("sync", ld["ptab_f"])))
        for c in range(8):
            load_stg(f"cw1_{c}", di["Cw1"][c], CH)
            stg_free[0] = b.v(lambda h, c=c: h.tensor_copy(cw1[:, c, :], stg[:, 0:CH]),
                              W(("sync", ld[f"cw1_{c}"])))
        for c in range(2):
            load_stg(f"cw2_{c}", di["Cw2"][c], 2)
            stg_free[0] = b.v(lambda h, c=c: h.tensor_copy(cw2[:, c, :], stg[:, 0:2]),
                              W(("sync", ld[f"cw2_{c}"])))
        for hhf in range(2):
            load_stg(f"binr{hhf}",
                     di["b_in_rep"][:, hhf * (D3 // 2):(hhf + 1) * (D3 // 2)],
                     D3 // 2)
            stg_free[0] = b.v(lambda h, hhf=hhf: h.tensor_copy(
                biasr[:, hhf * (D3 // 2):(hhf + 1) * (D3 // 2)], stg[:]),
                W(("sync", ld[f"binr{hhf}"])))
        gcols = sb("gcols", [P, 8], F32)
        load("gcols", gcols[:], di["gamma_cols"])

        H3 = D3 // 2
        for c in range(8):
            for hhf in range(2):
                load_stg(f"win_{c}_{hhf}", di["W_in"][c][:, hhf * H3:(hhf + 1) * H3],
                         H3)
                wv = b.v(lambda h, c=c, hhf=hhf: h.tensor_copy(
                    winp[:, c, hhf * H3:(hhf + 1) * H3], stg[:]),
                    W(("sync", ld[f"win_{c}_{hhf}"]), ("sync", ld["beta"])))
                for n3 in range(3):
                    n6 = hhf * 3 + n3
                    pdst = (pp0, pp1, pp2)[n6 // 2]
                    off = (n6 % 2) * 512
                    b.t(lambda h, c=c, n6=n6, pdst=pdst, off=off: h.matmul(
                        pdst[:, off:off + 512], zhiT[:, c, :],
                        winp[:, c, n6 * 512:(n6 + 1) * 512],
                        start=(c == 0), stop=(c == 7)),
                        W(("vector", wv)), inc=(n3 == 2))
                stg_free[0] = b.v(lambda h, c=c, hhf=hhf: h.tensor_scalar(
                    winp[:, c, hhf * H3:(hhf + 1) * H3], stg[:], gcols[:, c:c + 1],
                    None, ALU.mult),
                    W(("tensor", b.cnt["tensor"]), ("sync", ld["gcols"])))
        bias_mm = b.cnt["tensor"]
        for n6 in range(6):
            pdst = (pp0, pp1, pp2)[n6 // 2]
            off = (n6 % 2) * 512
            b.v(lambda h, n6=n6, pdst=pdst, off=off: h.tensor_add(
                biasr[:, n6 * 512:(n6 + 1) * 512], biasr[:, n6 * 512:(n6 + 1) * 512],
                pdst[:, off:off + 512]), W(("tensor", bias_mm)))
        pp_free = b.cnt["vector"]

        for c in range(8):
            load_stg(f"wo_{c}", di["W_out"][c], D)
            stg_free[0] = b.v(lambda h, c=c: h.tensor_copy(wout[:, c, :], stg[:, 0:D]),
                              W(("sync", ld[f"wo_{c}"])))

        kT = zloT
        kch = zb[:, 0:P]
        for kk, src in ((0, "k0c"), (1, "k1c")):
            for c in range(8):
                load_stg(f"k{kk}_{c}", di[src][c], P)
                cv = b.v(lambda h: h.tensor_copy(kch, stg[:, 0:P]),
                         W(("sync", ld[f"k{kk}_{c}"])))
                stg_free[0] = cv
                b.t(lambda h, c=c: h.transpose(ppT[:, c * P:(c + 1) * P], kch,
                                               ident[:]), W(("vector", cv)))
                b.v(lambda h, c=c: h.tensor_copy(kT[:, c, :], ppT[:, c * P:(c + 1) * P]),
                    W(("tensor", b.cnt["tensor"])))
            kt_done = b.cnt["vector"]
            for n2 in range(2):
                b.t(lambda h, n2=n2: h.matmul(
                    pp0[:, n2 * 512:(n2 + 1) * 512], fwd_re[:, 0, :],
                    kT[:].rearrange("p c k -> p (c k)")[:, n2 * 512:(n2 + 1) * 512],
                    start=True, stop=True),
                    W(("vector", max(kt_done, pp_free)), ("sync", ld["fwd_re"]),
                      ("sync", ld["fwd_imp"])) if n2 == 0 else (), inc=False)
            for n2 in range(2):
                b.t(lambda h, n2=n2: h.matmul(
                    pp1[:, n2 * 512:(n2 + 1) * 512], fwd_imp[:, 0, :],
                    kT[:].rearrange("p c k -> p (c k)")[:, n2 * 512:(n2 + 1) * 512],
                    start=True, stop=True), inc=(n2 == 1))
            tmm = b.cnt["tensor"]
            kt = ktab[kk]
            b.v(lambda h, kt=kt: h.tensor_copy(kt["A1"][:], pp0[:]), W(("tensor", tmm)))
            b.v(lambda h, kt=kt: h.tensor_copy(kt["B"][:], pp1[:]))
            b.v(lambda h, kt=kt: h.tensor_copy(kt["A2"][:], pp0[:]))
            b.v(lambda h, kt=kt: h.tensor_copy(kt["A2"][0:1, :], pp1[0:1, :]))
            pp_free = b.v(lambda h, kt=kt: h.memset(kt["B"][0:1, :], 0.0))

        state = dict(vready={}, uready={}, x2v={})

        def ln_stats(subs, c0, xw=None):
            n = len(subs)
            for i, s in enumerate(subs):
                sl = s % 10
                w = W(("sync", xw[s])) if xw else ()
                b.v(lambda h, i=i, sl=sl: h.bn_stats(statsA[:, i, 0:6],
                                                     x[:, sl, 0:512]), w)
                b.v(lambda h, i=i, sl=sl: h.bn_stats(statsA[:, i, 6:12],
                                                     x[:, sl, 512:D]))
            vsep()
            vsep()
            st4 = statsA[:, 0:n, :].rearrange("p s (t e) -> p s t e", t=4)
            mu4 = st4[:, :, :, 1]
            M4 = st4[:, :, :, 2]
            b.v(lambda h, n=n: h.tensor_reduce(colA[:, 0:n, c0 + 2], mu4, AXX,
                                               ALU.add))
            b.v(lambda h, n=n: h.tensor_mul(gw1[:, 0:n, 0:4], mu4, mu4))
            vsep()
            b.v(lambda h, n=n: h.tensor_reduce(colA[:, 0:n, c0 + 3],
                                               gw1[:, 0:n, 0:4], AXX, ALU.add))
            b.v(lambda h, n=n: h.tensor_reduce(colA[:, 0:n, c0 + 4], M4, AXX,
                                               ALU.add))
            b.v(lambda h, n=n: h.tensor_scalar(colA[:, 0:n, c0 + 2],
                                               colA[:, 0:n, c0 + 2], 0.25, None,
                                               ALU.mult))
            vsep()
            b.v(lambda h, n=n: h.tensor_scalar(colA[:, 0:n, c0 + 3],
                                               colA[:, 0:n, c0 + 3], 0.25, None,
                                               ALU.mult))
            vsep()
            b.v(lambda h, n=n: h.scalar_tensor_tensor(
                colA[:, 0:n, c0 + 3], colA[:, 0:n, c0 + 4], 1.0 / D,
                colA[:, 0:n, c0 + 3], ALU.mult, ALU.add))
            b.v(lambda h, n=n: h.tensor_mul(colA[:, 0:n, c0 + 4],
                                            colA[:, 0:n, c0 + 2],
                                            colA[:, 0:n, c0 + 2]))
            vsep()
            b.v(lambda h, n=n: h.tensor_sub(colA[:, 0:n, c0 + 3],
                                            colA[:, 0:n, c0 + 3],
                                            colA[:, 0:n, c0 + 4]))
            vsep()
            b.v(lambda h, n=n: h.tensor_scalar(colA[:, 0:n, c0 + 3],
                                               colA[:, 0:n, c0 + 3], EPS, None,
                                               ALU.add))
            vsep()
            rsqrt_newton(colA[:, 0:n, c0 + 1], colA[:, 0:n, c0 + 3], n)
            vsep()
            b.v(lambda h, n=n: h.tensor_mul(colA[:, 0:n, c0 + 0], colA[:, 0:n, c0 + 2],
                                            colA[:, 0:n, c0 + 1]))
            vsep()
            b.v(lambda h, n=n: h.tensor_scalar(colA[:, 0:n, c0 + 0],
                                               colA[:, 0:n, c0 + 0], -1.0, None,
                                               ALU.mult))

        def sigma_chain(dst, src, n):
            b.s(lambda h: h.activation(gw1[:, 0:n, :], src, AF.Abs, scale=invt[:]),
                W(("vector", b.cnt["vector"])))
            ssep()
            sS = b.s(lambda h: h.activation(gw1[:, 0:n, :], gw1[:, 0:n, :], AF.Exp,
                                            scale=-1.0))
            b.v(lambda h: h.tensor_scalar(gw1[:, 0:n, :], gw1[:, 0:n, :], 1.0, None,
                                          ALU.add), W(("scalar", sS)))
            vsep()
            b.v(lambda h: h.tensor_scalar(gw2[:, 0:n, :], gw1[:, 0:n, :], -8.0 / 17.0,
                                          24.0 / 17.0, ALU.mult, ALU.add))
            vsep()
            for _ in range(3):
                b.v(lambda h: h.tensor_mul(dst, gw1[:, 0:n, :], gw2[:, 0:n, :]))
                vsep()
                b.v(lambda h: h.tensor_scalar(dst, dst, -1.0, 2.0, ALU.mult, ALU.add))
                vsep()
                b.v(lambda h: h.tensor_mul(gw2[:, 0:n, :], gw2[:, 0:n, :], dst))
                vsep()
            b.v(lambda h: h.tensor_scalar(gw1[:, 0:n, :], src, 0.0, None, ALU.is_lt))
            b.v(lambda h: h.tensor_scalar(dst, gw2[:, 0:n, :], -2.0, 1.0, ALU.mult,
                                          ALU.add))
            vsep()
            b.v(lambda h: h.tensor_mul(gw1[:, 0:n, :], gw1[:, 0:n, :], dst))
            vsep()
            b.v(lambda h: h.tensor_add(dst, gw2[:, 0:n, :], gw1[:, 0:n, :]))

        def calib(dst, sig, n):
            b.v(lambda h: h.tensor_scalar(dst, sig, 0.0, knots[:, 0:1], ALU.mult,
                                          ALU.add))
            for j in range(7):
                b.v(lambda h, j=j: h.tensor_scalar(gw2[:, 0:n, :], sig, 7.0, -float(j),
                                                   ALU.mult, ALU.add))
                vsep()
                b.v(lambda h: h.tensor_scalar(gw2[:, 0:n, :], gw2[:, 0:n, :], 0.0, 1.0,
                                              ALU.max, ALU.min))
                vsep()
                b.v(lambda h, j=j: h.scalar_tensor_tensor(dst, gw2[:, 0:n, :],
                                                          dknot[:, j:j + 1], dst,
                                                          ALU.mult, ALU.add))
                vsep()

        def argmax8(idx_dst, vals, n, mx_dst):
            b.v(lambda h: h.tensor_reduce(mx_dst, vals, AXX, ALU.max))
            vsep()
            b.v(lambda h, n=n: h.tensor_tensor(gw1[:, 0:n, :], vals,
                                               mx_dst.broadcast_to([P, n, 8]),
                                               ALU.is_equal))
            vsep()
            b.v(lambda h, n=n: h.scalar_tensor_tensor(
                gw1[:, 0:n, :], gw1[:, 0:n, :], -BIG,
                io8B[:].unsqueeze(1).broadcast_to([P, n, 8]), ALU.mult, ALU.add))
            vsep()
            b.v(lambda h: h.tensor_reduce(idx_dst, gw1[:, 0:n, :], AXX, ALU.min))
            vsep()

        def emit_ffn(subs, dma_out=False, xw=None, tap=None):
            n = len(subs)
            ln_stats(subs, 0, xw=xw)
            statv = b.cnt["vector"]
            tprev = [b.cnt["tensor"]]
            vprev = [statv]
            for i, s in enumerate(subs):
                sl = s % 10
                sz = b.s(lambda h, i=i, sl=sl: h.activation(
                    zf[:], x[:, sl, :], AF.Identity, bias=colA[:, i, 0:1],
                    scale=colA[:, i, 1:2]),
                    W(("vector", vprev[0]), ("tensor", tprev[0])))
                vz = b.v(lambda h: h.tensor_copy(zb[:], zf[:]), W(("scalar", sz)))
                vlo = b.v(lambda h: h.tensor_sub(zlo[:], zf[:], zb[:]))
                vprev[0] = vlo
                for c in range(8):
                    b.t(lambda h, c=c: h.transpose(ppT[:, c * P:(c + 1) * P],
                                                   zb[:, c * P:(c + 1) * P], ident[:]),
                        W(("vector", vz)), inc=(c == 7))
                e1 = b.v(lambda h: h.tensor_copy(
                    zhiT[:].rearrange("p c k -> p (c k)"), ppT[:]),
                    W(("tensor", b.cnt["tensor"])))
                for c in range(8):
                    b.t(lambda h, c=c: h.transpose(ppT[:, c * P:(c + 1) * P],
                                                   zlo[:, c * P:(c + 1) * P], ident[:]),
                        W(("vector", e1)), inc=(c == 7))
                e2 = b.v(lambda h: h.tensor_copy(
                    zloT[:].rearrange("p c k -> p (c k)"), ppT[:]),
                    W(("tensor", b.cnt["tensor"])))
                for c in range(8):
                    b.t(lambda h, c=c: h.matmul(pps[:, 0:72], zhiT[:, c, :],
                                                sigc[:, c, :], start=(c == 0),
                                                stop=False),
                        W(("vector", e2)) if c == 0 else (), inc=False)
                    b.t(lambda h, c=c: h.matmul(pp2[:, 0:CH], zhiT[:, c, :],
                                                cw1[:, c, :], start=(c == 0),
                                                stop=False), inc=False)
                for c in range(8):
                    b.t(lambda h, c=c: h.matmul(pps[:, 0:72], zloT[:, c, :],
                                                sigc[:, c, :], start=False,
                                                stop=(c == 7)), inc=(c == 7))
                mm_done = b.cnt["tensor"]
                hbias = b.t(lambda h: h.matmul(pp2[:, 0:CH], ones_row[:], cb1r[:],
                                               start=False, stop=True),
                            W(("sync", ld["ones_row"]), ("sync", ld["cb1"])))
                sv = b.v(lambda h, i=i: h.tensor_copy(scoresA[:, i, :], pps[:, 0:72]),
                         W(("tensor", mm_done)))
                sh = b.s(lambda h: h.activation(hb[:], pp2[:, 0:CH], AF.Gelu),
                         W(("tensor", hbias)))
                for c in range(2):
                    b.t(lambda h, c=c: h.transpose(ppT[:, c * P:(c + 1) * P],
                                                   hb[:, c * P:(c + 1) * P], ident[:]),
                        W(("scalar", sh), ("vector", sv)), inc=(c == 1))
                e3 = b.v(lambda h: h.tensor_copy(
                    hT[:].rearrange("p c k -> p (c k)"), ppT[:, 0:2 * P]),
                    W(("tensor", b.cnt["tensor"])))
                for c in range(2):
                    b.t(lambda h, c=c: h.matmul(pps[:, 256:258], hT[:, c, :],
                                                cw2[:, c, :], start=(c == 0),
                                                stop=(c == 1)),
                        W(("vector", e3)) if c == 0 else (), inc=(c == 1))
                tprev[0] = b.cnt["tensor"]
                b.v(lambda h, i=i: h.tensor_add(compA[:, i, :], pps[:, 256:258],
                                                cb2r[:]), W(("tensor", tprev[0])))
            b.s(lambda h, n=n: h.activation(compA[:, 0:n, :], compA[:, 0:n, :],
                                            AF.Tanh), W(("vector", b.cnt["vector"])))
            tanh_done = b.cnt["scalar"]
            sigma_chain(gw0[:, 0:n, 0:8], scoresA[:, 0:n, 64:72], n)
            calib(gates[:, 0:n, 0:8], gw0[:, 0:n, 0:8], n)
            argmax8(rcol[:, 0:n, 0:1], gates[:, 0:n, 0:8], n, rcol[:, 0:n, 5:6])
            b.v(lambda h, n=n: h.tensor_scalar(spw[:, 0:n, 0:8], spw[:, 0:n, 0:8],
                                               0.0, None, ALU.mult))
            for c in range(8):
                b.v(lambda h, n=n, c=c: h.tensor_scalar(gw3[:, 0:n, 0:1],
                                                        rcol[:, 0:n, 0:1], float(c),
                                                        None, ALU.is_equal))
                vsep()
                b.v(lambda h, n=n, c=c: h.tensor_tensor(
                    gw2[:, 0:n, :], scoresA[:, 0:n, c * 8:(c + 1) * 8],
                    gw3[:, 0:n, 0:1].broadcast_to([P, n, 8]), ALU.mult))
                vsep()
                b.v(lambda h, n=n: h.tensor_add(spw[:, 0:n, 0:8], spw[:, 0:n, 0:8],
                                                gw2[:, 0:n, :]))
                vsep()
            sigma_chain(gw0[:, 0:n, 8:16], spw[:, 0:n, 0:8], n)
            calib(gates[:, 0:n, 8:16], gw0[:, 0:n, 8:16], n)
            argmax8(rcol[:, 0:n, 1:2], gates[:, 0:n, 8:16], n, rcol[:, 0:n, 5:6])
            b.v(lambda h, n=n: h.scalar_tensor_tensor(rcol[:, 0:n, 2:3],
                                                      rcol[:, 0:n, 0:1], 8.0,
                                                      rcol[:, 0:n, 1:2], ALU.mult,
                                                      ALU.add))
            vsep()
            b.v(lambda h, n=n: h.tensor_tensor(
                onehot[:, 0:n, :], io64[:].unsqueeze(1).broadcast_to([P, n, 64]),
                rcol[:, 0:n, 2:3].broadcast_to([P, n, 64]), ALU.is_equal))
            oh_done = b.cnt["vector"]
            for i, s in enumerate(subs):
                tt = b.t(lambda h, i=i: h.transpose(ppT[0:64, 0:P], onehot[:, i, :],
                                                    ident[:]),
                         W(("vector", oh_done)), inc=True)
                ev = b.v(lambda h: h.tensor_copy(ohT[:], ppT[0:64, 0:P]),
                         W(("tensor", tt)))
                pm = b.t(lambda h: h.matmul(pps[:, 0:65], ohT[:], ptab_b[:],
                                            start=True, stop=True),
                         W(("vector", ev)))
                b.v(lambda h, i=i: h.tensor_copy(parsA[:, i, :], pps[:, 0:65]),
                    W(("tensor", pm)))
            b.v(lambda h, n=n: h.tensor_tensor(
                spw[:, 0:n, :], parsA[:, 0:n, 0:16],
                compA[:, 0:n, 0:1].broadcast_to([P, n, 16]), ALU.mult),
                W(("scalar", tanh_done)))
            b.v(lambda h, n=n: h.tensor_tensor(
                spw2[:, 0:n, :], parsA[:, 0:n, 16:32],
                compA[:, 0:n, 1:2].broadcast_to([P, n, 16]), ALU.mult))
            b.v(lambda h, n=n: h.tensor_add(spw[:, 0:n, :], spw[:, 0:n, :],
                                            spw2[:, 0:n, :]))
            b.v(lambda h, n=n: h.tensor_add(spw[:, 0:n, :], spw[:, 0:n, :],
                                            parsA[:, 0:n, 32:48]))
            b.v(lambda h, n=n: h.tensor_relu(spw[:, 0:n, :], spw[:, 0:n, :]))
            b.v(lambda h, n=n: h.tensor_mul(spw[:, 0:n, :], spw[:, 0:n, :],
                                            parsA[:, 0:n, 48:64]))
            b.v(lambda h, n=n: h.tensor_reduce(rcol[:, 0:n, 3:4], spw[:, 0:n, :],
                                               AXX, ALU.add))
            vsep()
            b.v(lambda h, n=n: h.tensor_add(rcol[:, 0:n, 3:4], rcol[:, 0:n, 3:4],
                                            parsA[:, 0:n, 64:65]))
            vsep()
            b.v(lambda h, n=n: h.tensor_scalar(rcol[:, 0:n, 4:5], rcol[:, 0:n, 3:4],
                                               oscr[:], None, ALU.mult),
                W(("sync", ld["osc"])))
            vsep()
            mag_done = b.cnt["vector"]
            if tap == "x1":
                for i in range(n):
                    b.d(lambda h, i=i: h.dma_start(out=dbg["d_gates"][i],
                                                   in_=gates[:, i, :]),
                        W(("vector", mag_done)))
                    b.d(lambda h, i=i: h.dma_start(out=dbg["d_rcol"][i],
                                                   in_=rcol[:, i, :]))
                b.d(lambda h: h.dma_start(out=dbg["d_cols"],
                                          in_=colA[:].rearrange("p s k -> p (s k)")),
                    W(("vector", mag_done)))
                for i in range(n):
                    b.d(lambda h, i=i: h.dma_start(out=dbg["d_scores"][i],
                                                   in_=scoresA[:, i, :]),
                        W(("vector", mag_done)))
                    b.d(lambda h, i=i: h.dma_start(out=dbg["d_sig"][i],
                                                   in_=gw0[:, i, :]))
                    b.d(lambda h, i=i: h.dma_start(out=dbg["d_jsel"][i],
                                                   in_=rcol[:, i, 2:3]))
                    b.d(lambda h, i=i: h.dma_start(out=dbg["d_mag"][i],
                                                   in_=rcol[:, i, 3:4]))
                    b.d(lambda h, i=i: h.dma_start(out=dbg["d_comp"][i],
                                                   in_=compA[:, i, :]))
                    b.d(lambda h, i=i: h.dma_start(out=dbg["d_pars"][i],
                                                   in_=parsA[:, i, :]))
            yprev = [mag_done]
            for p0 in range(0, n, 2):
                pair = [(p0, subs[p0]), (p0 + 1, subs[p0 + 1])]
                ybufs = [ybuf, ybuf2]
                yvs = []
                for k2, (i, s) in enumerate(pair):
                    sl = s % 10
                    tt = b.t(lambda h, i=i: h.transpose(ppT[0:64, 0:P],
                                                        onehot[:, i, :], ident[:]),
                             W(("vector", max(mag_done, yprev[0]))), inc=True)
                    ev = b.v(lambda h: h.tensor_copy(ohT[:], ppT[0:64, 0:P]),
                             W(("tensor", tt)))
                    for n2 in range(2):
                        b.t(lambda h, n2=n2: h.matmul(
                            pp0[:, n2 * 512:(n2 + 1) * 512], ohT[:],
                            dirs_b[:, n2 * 512:(n2 + 1) * 512],
                            start=True, stop=True),
                            W(("vector", ev)) if n2 == 0 else (), inc=(n2 == 1))
                    dmm = b.cnt["tensor"]
                    sz = b.s(lambda h, i=i, sl=sl: h.activation(
                        zf[:], x[:, sl, :], AF.Identity, bias=colA[:, i, 0:1],
                        scale=colA[:, i, 1:2]), W(("vector", yprev[0])))
                    yv = b.v(lambda h, i=i, k2=k2: h.scalar_tensor_tensor(
                        ybufs[k2][:], pp0[:], rcol[:, i, 4:5], zf[:], ALU.mult,
                        ALU.add), W(("tensor", dmm), ("scalar", sz)))
                    b.v(lambda h, i=i, k2=k2: h.bn_stats(statsA[:, k2, 0:6],
                                                         ybufs[k2][:, 0:512]))
                    yv2 = b.v(lambda h, i=i, k2=k2: h.bn_stats(
                        statsA[:, k2, 6:12], ybufs[k2][:, 512:D]))
                    yprev[0] = yv2
                    yvs.append(yv)
                vsep()
                vsep()
                stA = statsA[:, 0, :].rearrange("p (t e) -> p t e", t=4)
                stB = statsA[:, 1, :].rearrange("p (t e) -> p t e", t=4)
                b.v(lambda h: h.tensor_reduce(colY[:, 0:1], stA[:, :, 1], AXX, ALU.add))
                b.v(lambda h: h.tensor_reduce(colY[:, 1:2], stB[:, :, 1], AXX, ALU.add))
                b.v(lambda h: h.tensor_mul(colY[:, 10:14], stA[:, :, 1], stA[:, :, 1]))
                b.v(lambda h: h.tensor_mul(colY[:, 14:18], stB[:, :, 1], stB[:, :, 1]))
                b.v(lambda h: h.tensor_reduce(colY[:, 2:3], colY[:, 10:14], AXX, ALU.add))
                b.v(lambda h: h.tensor_reduce(colY[:, 3:4], colY[:, 14:18], AXX, ALU.add))
                b.v(lambda h: h.tensor_reduce(colY[:, 4:5], stA[:, :, 2], AXX, ALU.add))
                b.v(lambda h: h.tensor_reduce(colY[:, 5:6], stB[:, :, 2], AXX, ALU.add))
                b.v(lambda h: h.tensor_scalar(colY[:, 0:1], colY[:, 0:1], -0.25, None,
                                              ALU.mult))
                b.v(lambda h: h.tensor_scalar(colY[:, 1:2], colY[:, 1:2], -0.25, None,
                                              ALU.mult))
                b.v(lambda h: h.tensor_scalar(colY[:, 2:3], colY[:, 2:3], 0.25, None,
                                              ALU.mult))
                b.v(lambda h: h.tensor_scalar(colY[:, 3:4], colY[:, 3:4], 0.25, None,
                                              ALU.mult))
                b.v(lambda h: h.scalar_tensor_tensor(colY[:, 2:3], colY[:, 4:5],
                                                     1.0 / D, colY[:, 2:3], ALU.mult,
                                                     ALU.add))
                b.v(lambda h: h.scalar_tensor_tensor(colY[:, 3:4], colY[:, 5:6],
                                                     1.0 / D, colY[:, 3:4], ALU.mult,
                                                     ALU.add))
                b.v(lambda h: h.tensor_mul(colY[:, 4:5], colY[:, 0:1], colY[:, 0:1]))
                b.v(lambda h: h.tensor_mul(colY[:, 5:6], colY[:, 1:2], colY[:, 1:2]))
                b.v(lambda h: h.tensor_sub(colY[:, 2:3], colY[:, 2:3], colY[:, 4:5]))
                b.v(lambda h: h.tensor_sub(colY[:, 3:4], colY[:, 3:4], colY[:, 5:6]))
                b.v(lambda h: h.tensor_scalar(colY[:, 2:3], colY[:, 2:3], EPS, None,
                                              ALU.add))
                b.v(lambda h: h.tensor_scalar(colY[:, 3:4], colY[:, 3:4], EPS, None,
                                              ALU.add))
                vsep()
                rsqrt_newton(colY[:, 6:8], colY[:, 2:4], 2)
                b.v(lambda h: h.tensor_mul(colY[:, 8:9], colY[:, 0:1], colY[:, 6:7]))
                vr = b.v(lambda h: h.tensor_mul(colY[:, 9:10], colY[:, 1:2],
                                                colY[:, 7:8]))
                for k2, (i, s) in enumerate(pair):
                    sl = s % 10
                    sl2 = b.s(lambda h, k2=k2: h.activation(
                        zf[:], ybufs[k2][:], AF.Identity, bias=colY[:, 8 + k2:9 + k2],
                        scale=colY[:, 6 + k2:7 + k2]),
                        W(("vector", vr if k2 == 0 else yprev[0])))
                    b.v(lambda h: h.tensor_mul(zf[:], zf[:], g05[:]),
                        W(("scalar", sl2)))
                    b.v(lambda h: h.tensor_add(zf[:], zf[:], bt05[:]))
                    xu = b.v(lambda h, sl=sl: h.tensor_add(x[:, sl, :], x[:, sl, :],
                                                           zf[:]))
                    state["x2v"][s] = xu
                    yprev[0] = xu
                    if dma_out:
                        b.d(lambda h, s=s, sl=sl: h.dma_start(out=xout[s - 2],
                                                              in_=x[:, sl, :]),
                            W(("vector", xu)))
                    if tap == "x1":
                        b.d(lambda h, i=i, sl=sl: h.dma_start(out=dbg["d_x1"][i],
                                                              in_=x[:, sl, :]),
                            W(("vector", xu)))
                    if tap == "x2o":
                        b.d(lambda h, i=i, sl=sl: h.dma_start(out=dbg["d_x2"][i],
                                                              in_=x[:, sl, :]),
                            W(("vector", xu)))

        def conv_window(s, ring, vwait, kt, out_cb):
            sw = state.get("spec_evac", 0)
            for mt, mats, pdst in ((0, fwd_re, pp0), (1, fwd_imp, pp1)):
                for kc, wslot in ((0, (s - 1) % 2), (1, s % 2)):
                    for n2 in range(2):
                        b.t(lambda h, mats=mats, kc=kc, n2=n2, pdst=pdst, wslot=wslot:
                            h.matmul(pdst[:, n2 * 512:(n2 + 1) * 512], mats[:, kc, :],
                                     ring[:, wslot, n2 * 512:(n2 + 1) * 512],
                                     start=(kc == 0), stop=(kc == 1)),
                            W(("vector", vwait), ("scalar", sw))
                            if (mt == 0 and kc == 0 and n2 == 0)
                            else (), inc=(mt == 1 and kc == 1 and n2 == 1))
            fwd_done = b.cnt["tensor"]
            se1 = b.s(lambda h: h.activation(fxr, pp0[:], AF.Identity),
                      W(("tensor", fwd_done)))
            se2 = b.s(lambda h: h.activation(fxi, pp1[:], AF.Identity))
            state["spec_evac"] = se2
            b.v(lambda h, kt=kt: h.tensor_tensor(fq1[:], fxr, kt["A1"][:],
                                                 ALU.mult),
                W(("scalar", se1)))
            b.v(lambda h, kt=kt: h.tensor_tensor(fqt[:], fxi, kt["B"][:], ALU.mult),
                W(("scalar", se2)))
            b.v(lambda h: h.tensor_sub(fq1[:], fq1[:], fqt[:]))
            b.v(lambda h, kt=kt: h.tensor_tensor(fq2[:], fxr, kt["B"][:], ALU.mult))
            b.v(lambda h, kt=kt: h.tensor_tensor(fqt[:], fxi, kt["A2"][:],
                                                 ALU.mult))
            fm = b.v(lambda h: h.tensor_add(fq2[:], fq2[:], fqt[:]))
            for n2 in range(2):
                b.t(lambda h, n2=n2: h.matmul(pp2[:, n2 * 512:(n2 + 1) * 512],
                                              inv_c[:], fq1[:, n2 * 512:(n2 + 1) * 512],
                                              start=True, stop=False),
                    W(("vector", fm)) if n2 == 0 else (), inc=False)
            for n2 in range(2):
                b.t(lambda h, n2=n2: h.matmul(pp2[:, n2 * 512:(n2 + 1) * 512],
                                              inv_s[:], fq2[:, n2 * 512:(n2 + 1) * 512],
                                              start=False, stop=True), inc=(n2 == 1))
            return b.v(out_cb, W(("tensor", b.cnt["tensor"])))

        def emit_attn(subs, tap=None):
            n = len(subs)
            ln_stats(subs, 8)
            b.v(lambda h, n=n: h.tensor_scalar(colA[:, 0:n, 11], colA[:, 0:n, 11],
                                               -EPS, None, ALU.add))
            b.v(lambda h, n=n: h.tensor_mul(colA[:, 0:n, 12], colA[:, 0:n, 9],
                                            colA[:, 0:n, 9]))
            vsep()
            b.v(lambda h, n=n: h.tensor_mul(colA[:, 0:n, 12], colA[:, 0:n, 12],
                                            colA[:, 0:n, 11]))
            vsep()
            b.v(lambda h, n=n: h.tensor_scalar(colA[:, 0:n, 12], colA[:, 0:n, 12],
                                               EPS, None, ALU.add))
            vsep()
            rsqrt_newton(colA[:, 0:n, 13], colA[:, 0:n, 12], n)
            vsep()
            b.v(lambda h, n=n: h.tensor_mul(colA[:, 0:n, 10], colA[:, 0:n, 9],
                                            colA[:, 0:n, 13]))
            b.v(lambda h, n=n: h.tensor_mul(colA[:, 0:n, 12], colA[:, 0:n, 8],
                                            colA[:, 0:n, 13]))
            b.v(lambda h, n=n: h.tensor_scalar(colA[:, 0:n, 11], colA[:, 0:n, 9],
                                               1.0, None, ALU.add))
            stat_done = b.cnt["vector"]
            tprev = [b.cnt["tensor"]]
            for i, s in enumerate(subs):
                sl = s % 10
                sz = b.s(lambda h, i=i, sl=sl: h.activation(
                    zb[:], x[:, sl, :], AF.Identity, bias=colA[:, i, 12:13],
                    scale=colA[:, i, 10:11]),
                    W(("vector", stat_done), ("tensor", tprev[0])))
                for c in range(8):
                    b.t(lambda h, c=c: h.transpose(ppT[:, c * P:(c + 1) * P],
                                                   zb[:, c * P:(c + 1) * P],
                                                   ident[:]),
                        W(("scalar", sz)), inc=(c == 7))
                e1 = b.v(lambda h: h.tensor_copy(
                    zhiT[:].rearrange("p c k -> p (c k)"), ppT[:]),
                    W(("tensor", b.cnt["tensor"])))
                for c in range(8):
                    for n6 in range(6):
                        pdst = (pp0, pp1, pp2)[n6 // 2]
                        off = (n6 % 2) * 512
                        b.t(lambda h, c=c, n6=n6, pdst=pdst, off=off: h.matmul(
                            pdst[:, off:off + 512], zhiT[:, c, :],
                            winp[:, c, n6 * 512:(n6 + 1) * 512],
                            start=(c == 0), stop=(c == 7)),
                            W(("vector", e1)) if (c == 0 and n6 == 0) else (),
                            inc=(c == 7 and n6 == 5))
                proj_done = b.cnt["tensor"]
                b.v(lambda h, s=s: h.tensor_add(vring[:, s % 2, :], pp0[:],
                                                biasr[:, 0:D]),
                    W(("tensor", proj_done)))
                if s < 2:
                    b.v(lambda h, s=s: h.tensor_scalar(vring[:, s % 2, :],
                                                       vring[:, s % 2, :],
                                                       maskc[:, s:s + 1], None,
                                                       ALU.mult),
                        W(("sync", ld["mask"])))
                b.v(lambda h: h.tensor_add(g1c[:], pp1[:], biasr[:, D:2 * D]))
                b.v(lambda h: h.tensor_add(g2c[:], pp2[:], biasr[:, 2 * D:3 * D]))
                state["vready"][s] = b.cnt["vector"]
                if tap and s == 2:
                    b.d(lambda h, s=s: h.dma_start(out=dbg["d_v"][0],
                                                   in_=vring[:, s % 2, :]),
                        W(("vector", state["vready"][s])))
                    b.d(lambda h: h.dma_start(out=dbg["d_g1"][0], in_=g1c[:]))
                if s >= 1:
                    uv = conv_window(
                        s, vring, max(state["vready"][s],
                                      state["vready"].get(s - 1, 0)), ktab[0],
                        lambda h, s=s: h.tensor_tensor(uring[:, s % 2, :], pp2[:],
                                                       g1c[:], ALU.mult))
                    state["uready"][s] = uv
                    if tap and s == 2:
                        b.d(lambda h, s=s: h.dma_start(out=dbg["d_u"][0],
                                                       in_=uring[:, s % 2, :]),
                            W(("vector", uv)))
                if s >= 2:
                    yv = conv_window(
                        s, uring, max(state["uready"][s],
                                      state["uready"].get(s - 1, 0)), ktab[1],
                        lambda h: h.tensor_tensor(ygc[:], pp2[:], g2c[:], ALU.mult))
                    if tap and s == 2:
                        b.d(lambda h: h.dma_start(out=dbg["d_yg"][0], in_=ygc[:]),
                            W(("vector", yv)))
                    for c in range(8):
                        b.t(lambda h, c=c: h.transpose(ppT[:, c * P:(c + 1) * P],
                                                       ygc[:, c * P:(c + 1) * P],
                                                       ident[:]),
                            W(("vector", yv)), inc=(c == 7))
                    e2 = b.v(lambda h: h.tensor_copy(
                        zloT[:].rearrange("p c k -> p (c k)"), ppT[:]),
                        W(("tensor", b.cnt["tensor"])))
                    for c in range(8):
                        for n2 in range(2):
                            b.t(lambda h, c=c, n2=n2: h.matmul(
                                pp0[:, n2 * 512:(n2 + 1) * 512], zloT[:, c, :],
                                wout[:, c, n2 * 512:(n2 + 1) * 512],
                                start=(c == 0), stop=(c == 7)),
                                W(("vector", e2)) if (c == 0 and n2 == 0) else (),
                                inc=(c == 7 and n2 == 1))
                    wo_done = b.cnt["tensor"]
                    b.v(lambda h: h.tensor_add(ybuf[:], pp0[:], boutr[:]),
                        W(("tensor", wo_done)))
                    b.v(lambda h, i=i: h.tensor_scalar(ybuf[:], ybuf[:],
                                                       colA[:, i, 8:9], None,
                                                       ALU.add))
                    xu = b.v(lambda h, i=i, sl=sl: h.scalar_tensor_tensor(
                        x[:, sl, :], x[:, sl, :], colA[:, i, 11:12], ybuf[:],
                        ALU.mult, ALU.add))
                    state["x2v"][s] = xu
                tprev[0] = b.cnt["tensor"]

        emit_ffn(HALF1, xw=x_loaded, tap=("x1" if debug else None))
        emit_attn(HALF1, tap=debug)
        emit_ffn(HALF1[2:], dma_out=True, tap=("x2o" if debug else None))
        x_loaded2 = {}
        reload_wait = b.cnt["sync"]
        for s in HALF2:
            x_loaded2[s] = b.d(
                lambda h, s=s: h.dma_start(out=x[:, s % 10, :], in_=di["xin"][s]),
                W(("vector", b.cnt["vector"]), ("sync", reload_wait)))
        emit_ffn(HALF2, xw=x_loaded2)
        emit_attn(HALF2)
        emit_ffn(HALF2, dma_out=True)

        b.g(lambda h: None, W(("sync", b.cnt["sync"])))
        b.run(block, sems)

    return nc



def _in_maps(inputs):
    f32 = np.float32
    bf16 = ml_dtypes.bfloat16
    x = np.ascontiguousarray(np.asarray(inputs["L0_attn_x"]), dtype=f32)
    W_in = np.ascontiguousarray(np.asarray(inputs["L0_attn_W_in"], f32).reshape(8, P, D3))
    W_out = np.ascontiguousarray(np.asarray(inputs["L0_attn_W_out"], f32).reshape(8, P, D))
    Cw1 = np.ascontiguousarray(np.asarray(inputs["L0_ffn_compress_w1"], f32).reshape(8, P, CH))
    Cw2 = np.ascontiguousarray(np.asarray(inputs["L0_ffn_compress_w2"], f32).reshape(2, P, 2))
    k0c = np.ascontiguousarray(np.asarray(inputs["L0_attn_k0"], f32)[0].reshape(8, P, P))
    k1c = np.ascontiguousarray(np.asarray(inputs["L0_attn_k1"], f32)[0].reshape(8, P, P))
    gamma = np.asarray(inputs["L0_attn_gamma"], f32)
    beta = np.asarray(inputs["L0_attn_beta"], f32)
    gamma_cols = np.ascontiguousarray(gamma.reshape(8, P).T)
    beta_rep = np.ascontiguousarray(
        np.broadcast_to(beta.reshape(8, P, 1), (8, P, P)).transpose(1, 0, 2)
    ).astype(bf16)
    b_in_rep = np.ascontiguousarray(
        np.broadcast_to(np.asarray(inputs["L0_attn_b_in"], f32)[None, :], (P, D3)))
    b_out_rep = np.ascontiguousarray(
        np.broadcast_to(np.asarray(inputs["L0_attn_b_out"], f32)[None, :], (P, D)))
    cb1_row = np.ascontiguousarray(
        np.asarray(inputs["L0_ffn_compress_b1"], f32)[None, :]).astype(bf16)
    cb2_rep = np.ascontiguousarray(
        np.broadcast_to(np.asarray(inputs["L0_ffn_compress_b2"], f32)[None, :], (P, 2)))
    gf_rep = np.ascontiguousarray(
        np.broadcast_to(np.asarray(inputs["L0_ffn_norm_gamma"], f32)[None, :], (P, D)))
    bf_rep = np.ascontiguousarray(
        np.broadcast_to(np.asarray(inputs["L0_ffn_norm_beta"], f32)[None, :], (P, D)))
    dirs = np.ascontiguousarray(np.asarray(inputs["L0_ffn_directions"], f32))
    ptab = np.ascontiguousarray(np.concatenate([
        np.asarray(inputs["L0_ffn_spline_w1"], f32).reshape(64, 32),
        np.asarray(inputs["L0_ffn_spline_b1"], f32),
        np.asarray(inputs["L0_ffn_spline_w2"], f32).reshape(64, 16),
        np.asarray(inputs["L0_ffn_spline_b2"], f32).reshape(64, 1)], axis=1))
    knots_rep = np.ascontiguousarray(
        np.broadcast_to(np.asarray(inputs["L0_ffn_knot_values"], f32)[None, :], (P, 8)))
    temp_rep = np.ascontiguousarray(
        np.broadcast_to(np.asarray(inputs["L0_ffn_temperature"], f32)[None, :], (P, 1)))
    osc_rep = np.ascontiguousarray(
        np.broadcast_to(np.asarray(inputs["L0_ffn_output_scale"], f32)[None, :], (P, 1)))
    fwd_re, fwd_imp, inv_c, inv_s = _dft_tables()
    ident = np.eye(P, dtype=bf16)
    io64 = np.ascontiguousarray(
        np.broadcast_to(np.arange(64, dtype=f32)[None, :], (P, 64)))
    io8B = np.ascontiguousarray(
        np.broadcast_to((np.arange(8).astype(np.float64) + BIG).astype(f32)[None, :],
                        (P, 8)))
    ones_row = np.ones((1, P), bf16)

    shared = dict(W_in=W_in, W_out=W_out, Cw1=Cw1, Cw2=Cw2,
                  sigs_raw=np.ascontiguousarray(np.asarray(inputs["L0_ffn_signatures_raw"], f32)),
                  k0c=k0c, k1c=k1c, gamma_cols=gamma_cols, beta_rep=beta_rep,
                  b_in_rep=b_in_rep, b_out_rep=b_out_rep, cb1_row=cb1_row,
                  cb2_rep=cb2_rep, gf_rep=gf_rep, bf_rep=bf_rep, dirs=dirs,
                  ptab=ptab, knots_rep=knots_rep, temp_rep=temp_rep,
                  osc_rep=osc_rep, fwd_re=fwd_re, fwd_imp=fwd_imp, inv_c=inv_c,
                  inv_s=inv_s, ident=ident, io64=io64, io8B=io8B,
                  ones_row=ones_row)

    maps = []
    for core in range(8):
        bseq, half = core // 2, core % 2
        start = half * 2048
        xin = np.zeros((S_TOT, P, D), f32)
        if half == 1:
            xin[:] = x[bseq, start - 256:start + 2048].reshape(S_TOT, P, D)
        else:
            xin[2:] = x[bseq, start:start + 2048].reshape(16, P, D)
        m = dict(shared)
        m["xin"] = xin
        m["halo_mask"] = np.full((P, 2), 1.0 if half == 1 else 0.0, f32)
        maps.append(m)
    return maps


def kernel(**inputs):
    maps = _in_maps(inputs)
    nc = build_nc(debug=_DEBUG)
    res = run_bass_kernel_spmd(nc, maps, core_ids=list(range(8)))
    out = np.empty((4, 4096, 1024), np.float32)
    for core in range(8):
        bseq, half = core // 2, core % 2
        xo = res.results[core]["xout"].reshape(2048, D)
        out[bseq, half * 2048:(half + 1) * 2048] = xo
    return out


# version 03
# speedup vs baseline: 1.0681x; 1.0296x over previous
import contextlib

import numpy as np
import ml_dtypes

import concourse.bass as bass
import concourse.mybir as mybir
from concourse.bass_utils import run_bass_kernel_spmd

F32 = mybir.dt.float32
BF16 = mybir.dt.bfloat16
I32 = mybir.dt.int32
AF = mybir.ActivationFunctionType
ALU = mybir.AluOpType
AXX = mybir.AxisListType.X

P = 128
D = 1024
D3 = 3072
CH = 256
S_TOT = 18
HALF1 = list(range(0, 10))
HALF2 = list(range(10, 18))
EPS = 1e-5
BIG = 1024.0

_DEBUG = False


def _dft_tables():
    n = 256
    j = np.arange(n)[:, None].astype(np.float64)
    k = np.arange(128)[None, :].astype(np.float64)
    ang = 2.0 * np.pi * (j * k) / n
    fwd_re = np.cos(ang)
    fwd_imp = -np.sin(ang)
    fwd_imp[:, 0] = np.cos(np.pi * j[:, 0])
    tau = np.arange(128)[None, :].astype(np.float64)
    jj = 128.0 + tau
    kk = np.arange(128)[:, None].astype(np.float64)
    c = np.cos(2.0 * np.pi * kk * jj / n) * (2.0 / n)
    c[0, :] = 1.0 / n
    s = -np.sin(2.0 * np.pi * kk * jj / n) * (2.0 / n)
    s[0, :] = np.cos(np.pi * jj[0]) * (1.0 / n)
    bf = ml_dtypes.bfloat16
    return (np.ascontiguousarray(fwd_re.reshape(2, 128, 128).transpose(1, 0, 2)).astype(bf),
            np.ascontiguousarray(fwd_imp.reshape(2, 128, 128).transpose(1, 0, 2)).astype(bf),
            c.astype(bf), s.astype(bf))


class B:
    ENGS = ("tensor", "vector", "scalar", "gpsimd", "sync")

    def __init__(self):
        self.streams = {e: [] for e in self.ENGS}
        self.cnt = {e: 0 for e in self.ENGS}

    def op(self, eng, fn, waits=(), inc=True):
        step = 16 if eng == "sync" else 1
        if inc:
            self.cnt[eng] += step
        self.streams[eng].append((fn, tuple(waits), inc, self.cnt[eng]))
        return self.cnt[eng]

    def t(self, fn, waits=(), inc=True):
        return self.op("tensor", fn, waits, inc)

    def v(self, fn, waits=(), inc=True):
        return self.op("vector", fn, waits, inc)

    def s(self, fn, waits=(), inc=True):
        return self.op("scalar", fn, waits, inc)

    def g(self, fn, waits=(), inc=True):
        return self.op("gpsimd", fn, waits, inc)

    def d(self, fn, waits=(), inc=True):
        return self.op("sync", fn, waits, inc)

    def run(self, block, sems):
        def make_body(eng_name):
            stream = self.streams[eng_name]
            step = 16 if eng_name == "sync" else 1

            def body(h):
                waited = {}
                for fn, waits, inc, val in stream:
                    for we, wv in waits:
                        if waited.get(we, -1) < wv:
                            h.wait_ge(sems[we], wv)
                            waited[we] = wv
                    inst = fn(h)
                    if inc:
                        if inst is None:
                            inst = h.nop()
                        inst.then_inc(sems[eng_name], step)
            return body

        block.tensor(make_body("tensor"))
        block.vector(make_body("vector"))
        block.scalar(make_body("scalar"))
        block.gpsimd(make_body("gpsimd"))
        block.sync(make_body("sync"))


def build_nc(debug=False):
    nc = bass.Bass("TRN2", target_bir_lowering=False)
    di = {}

    def din(name, shape, dt=F32):
        di[name] = nc.dram_tensor(name, shape, dt, kind="ExternalInput").ap()
        return di[name]

    din("xin", [S_TOT, P, D])
    din("halo_mask", [P, 2])
    din("W_in", [8, P, D3])
    din("W_out", [8, P, D])
    din("Cw1", [8, P, CH])
    din("Cw2", [2, P, 2])
    din("sigs_raw", [64, D])
    din("k0c", [8, P, P])
    din("k1c", [8, P, P])
    din("gamma_cols", [P, 8])
    din("beta_rep", [P, 8, P], BF16)
    din("b_in_rep", [P, D3])
    din("b_out_rep", [P, D])
    din("cb1_row", [1, CH], BF16)
    din("cb2_rep", [P, 2])
    din("gf_rep", [P, D])
    din("bf_rep", [P, D])
    din("dirs", [64, D])
    din("ptab", [64, 65])
    din("knots_rep", [P, 8])
    din("temp_rep", [P, 1])
    din("osc_rep", [P, 1])
    din("fwd_re", [P, 2, P], BF16)
    din("fwd_imp", [P, 2, P], BF16)
    din("inv_c", [P, P], BF16)
    din("inv_s", [P, P], BF16)
    din("ident", [P, P], BF16)
    din("io64", [P, 64])
    din("io8B", [P, 8])
    din("ones_row", [1, P], BF16)

    xout = nc.dram_tensor("xout", [16, P, D], F32, kind="ExternalOutput").ap()

    dbg = {}
    if debug:
        for nm, shape, dt_ in [("d_scores", [10, P, 72], F32), ("d_sig", [10, P, 16], F32),
                               ("d_jsel", [10, P, 1], F32), ("d_mag", [10, P, 1], F32),
                               ("d_x1", [10, P, D], F32), ("d_v", [3, P, D], BF16),
                               ("d_g1", [3, P, D], BF16), ("d_u", [3, P, D], BF16),
                               ("d_yg", [3, P, D], BF16), ("d_x2", [10, P, D], F32),
                               ("d_comp", [10, P, 2], F32), ("d_pars", [10, P, 65], F32),
                               ("d_cols", [P, 160], F32), ("d_gates", [10, P, 16], F32),
                               ("d_rcol", [10, P, 8], F32), ("d_ybuf", [P, D], F32),
                               ("d_lnb", [P, D], BF16), ("d_tb", [P, D], BF16),
                               ("d_colY", [P, 8], F32)]:
            dbg[nm] = nc.dram_tensor(nm, shape, dt_, kind="ExternalOutput").ap()

    b = B()

    with contextlib.ExitStack() as ctx:
        def sb(name, shape, dt):
            return ctx.enter_context(nc.sbuf_tensor(name, shape, dt))

        x = sb("x", [P, 10, D], F32)
        winp = sb("winp", [P, 8, D3], BF16)
        wout = sb("wout", [P, 8, D], BF16)
        cw1 = sb("cw1", [P, 8, CH], BF16)
        cw2 = sb("cw2", [P, 2, 2], BF16)
        sigc = sb("sigc", [P, 8, 72], BF16)
        biasr = sb("biasr", [P, D3], BF16)
        cb1r = sb("cb1r", [1, CH], BF16)
        cb2r = sb("cb2r", [P, 2], F32)
        boutr = sb("boutr", [P, D], F32)
        g05 = sb("g05", [P, D], F32)
        bt05 = sb("bt05", [P, D], F32)
        dirs_b = sb("dirs_b", [64, D], BF16)
        ptab_b = sb("ptab_b", [64, 65], BF16)
        ktab = {kk: dict(A1=sb(f"A1_{kk}", [P, D], BF16),
                         B=sb(f"B_{kk}", [P, D], BF16),
                         A2=sb(f"A2_{kk}", [P, D], BF16)) for kk in (0, 1)}
        fwd_re = sb("fwd_re_s", [P, 2, P], BF16)
        fwd_imp = sb("fwd_imp_s", [P, 2, P], BF16)
        inv_c = sb("inv_c_s", [P, P], BF16)
        inv_s = sb("inv_s_s", [P, P], BF16)
        ident = sb("ident_s", [P, P], BF16)
        io64 = sb("io64_s", [P, 64], F32)
        io8B = sb("io8B_s", [P, 8], F32)
        ones_row = sb("ones_row_s", [1, P], BF16)
        knots = sb("knots_s", [P, 8], F32)
        dknot = sb("dknot_s", [P, 7], F32)
        invt = sb("invt_s", [P, 1], F32)
        oscr = sb("oscr_s", [P, 1], F32)
        maskc = sb("maskc", [P, 2], F32)

        stg = sb("stg", [P, D3 // 2], F32)
        zf = sb("zf", [P, D], F32)
        zb = sb("zb", [P, D], BF16)
        zlo = sb("zlo", [P, D], BF16)
        zhiT = sb("zhiT", [P, 8, P], BF16)
        zloT = sb("zloT", [P, 8, P], BF16)
        hb = sb("hb", [P, CH], BF16)
        hT = sb("hT", [P, 2, P], BF16)
        ohT = sb("ohT", [64, P], BF16)

        NS = 10
        scoresA = sb("scoresA", [P, NS, 72], F32)
        statsA = sb("statsA", [P, NS, 12], F32)
        colA = sb("colA", [P, NS, 16], F32)
        gw0 = sb("gw0", [P, NS, 16], F32)
        gw1 = sb("gw1", [P, NS, 8], F32)
        gw2 = sb("gw2", [P, NS, 8], F32)
        gw3 = sb("gw3", [P, NS, 8], F32)
        gates = sb("gates", [P, NS, 16], F32)
        rcol = sb("rcol", [P, NS, 8], F32)
        onehot = sb("onehot", [P, NS, 64], BF16)
        compA = sb("compA", [P, NS, 2], F32)
        spw = sb("spw", [P, NS, 16], F32)
        spw2 = sb("spw2", [P, NS, 16], F32)
        parsA = sb("parsA", [P, NS, 65], F32)
        ybuf = sb("ybuf", [P, D], F32)
        ybuf2 = sb("ybuf2", [P, D], F32)

        vring = sb("vring", [P, 2, D], BF16)
        uring = sb("uring", [P, 2, D], BF16)
        g1c = sb("g1c", [P, D], BF16)
        g2c = sb("g2c", [P, D], BF16)
        ygc = sb("ygc", [P, D], BF16)
        fq1 = sb("fq1", [P, D], BF16)
        fq2 = sb("fq2", [P, D], BF16)
        fqt = sb("fqt", [P, D], BF16)
        fxr = parsA[:].rearrange("p s k -> p (s k)")[:, 0:512].bitcast(BF16)
        fxi = scoresA[:].rearrange("p s k -> p (s k)")[:, 0:512].bitcast(BF16)

        pp0 = ctx.enter_context(nc.psum_tensor("pp0", [P, D], F32))
        pp1 = ctx.enter_context(nc.psum_tensor("pp1", [P, D], F32))
        pp2 = ctx.enter_context(nc.psum_tensor("pp2", [P, D], F32))
        ppT = ctx.enter_context(nc.psum_tensor("ppT", [P, D], BF16))
        pps = ctx.enter_context(nc.psum_tensor("pps", [P, 512], F32))

        sems = {e: ctx.enter_context(nc.semaphore(f"sem_{e}")) for e in B.ENGS}
        block = ctx.enter_context(nc.Block())

        def W(*pairs):
            return tuple(pairs)

        def vsep():
            b.v(lambda h: h.tensor_scalar(sepA[:], sepA[:], 0.0, None, ALU.mult),
                inc=False)

        def ssep():
            b.s(lambda h: h.activation(sepA[:], sepA[:], AF.Identity), inc=False)


        ld = {}
        stg_free = [0]

        def load(name, out_ap, in_ap, waits=()):
            ld[name] = b.d(lambda h, o=out_ap, i=in_ap: h.dma_start(out=o, in_=i),
                           waits)
            return ld[name]

        def load_stg(name, in_ap, cols, rows=P):
            return load(name, stg[:rows, 0:cols], in_ap, W(("vector", stg_free[0])))

        load("ident", ident[:], di["ident"])
        load("io64", io64[:], di["io64"])
        load("io8B", io8B[:], di["io8B"])
        load("ones_row", ones_row[:], di["ones_row"])
        load("knots", knots[:], di["knots_rep"])
        load("temp", invt[:], di["temp_rep"])
        load("osc", oscr[:], di["osc_rep"])
        load("mask", maskc[:], di["halo_mask"])
        load("fwd_re", fwd_re[:], di["fwd_re"])
        load("fwd_imp", fwd_imp[:], di["fwd_imp"])
        load("inv_c", inv_c[:], di["inv_c"])
        load("inv_s", inv_s[:], di["inv_s"])
        load("cb1", cb1r[:], di["cb1_row"])
        load("cb2", cb2r[:], di["cb2_rep"])
        load("bout", boutr[:], di["b_out_rep"])
        load("beta", zhiT[:], di["beta_rep"])
        x_loaded = {}
        for s in range(10):
            x_loaded[s] = load(f"x{s}", x[:, s, :], di["xin"][s])

        sigs16 = ygc
        load_stg("sigs_raw", di["sigs_raw"], D, rows=64)
        b.v(lambda h: h.tensor_scalar(zf[:64, :], stg[:64, 0:D], 0.3, None, ALU.is_gt),
            W(("sync", ld["sigs_raw"])))
        b.v(lambda h: h.tensor_scalar(ybuf[:64, :], stg[:64, 0:D], -0.3, None, ALU.is_lt))
        b.v(lambda h: h.tensor_sub(zf[:64, :], zf[:64, :], ybuf[:64, :]))
        sig_cast = b.v(lambda h: h.tensor_copy(sigs16[:64, :], zf[:64, :]))
        stg_free[0] = sig_cast
        for c in range(8):
            b.t(lambda h, c=c: h.transpose(ppT[:, c * 64:(c + 1) * 64],
                                           sigs16[:64, c * P:(c + 1) * P],
                                           ident[:64, :64]),
                W(("vector", sig_cast), ("sync", ld["ident"])), inc=(c == 7))
        tp = b.cnt["tensor"]
        for c in range(8):
            b.v(lambda h, c=c: h.tensor_copy(sigc[:, c, 0:64], ppT[:, c * 64:(c + 1) * 64]),
                W(("tensor", tp)))
        cs_v = sb("cs_v", [P, 8, 8], F32)
        cs_w = sb("cs_w", [P, 8, 8], F32)
        b.v(lambda h: h.tensor_reduce(
            cs_v[:],
            sigc[:, :, 0:64].rearrange("p c (g l) -> p c g l", g=8), AXX, ALU.add))
        vsep()
        b.v(lambda h: h.tensor_scalar(cs_w[:], cs_v[:], 0.0, None, ALU.is_gt))
        vsep()
        b.v(lambda h: h.tensor_scalar(cs_v[:], cs_v[:], 0.0, None, ALU.is_lt))
        vsep()
        b.v(lambda h: h.tensor_sub(cs_w[:], cs_w[:], cs_v[:]))
        for c in range(8):
            b.v(lambda h, c=c: h.tensor_copy(sigc[:, c, 64:72], cs_w[:, c, :]))

        def rsqrt_newton(dst, src, k, waits=()):
            b.v(lambda h: h.tensor_copy(rsA[:, 0:k], src), waits)
            vsep()
            vsep()
            b.v(lambda h: h.tensor_scalar(rsB[:].bitcast(I32), rsA[:].bitcast(I32), 1,
                                          None, ALU.logical_shift_right))
            vsep()
            b.v(lambda h: h.tensor_scalar(rsB[:].bitcast(I32), rsB[:].bitcast(I32), -1,
                                          0x5F3759DF, ALU.mult, ALU.add))
            vsep()
            for _ in range(3):
                b.v(lambda h: h.tensor_mul(rsC[:], rsB[:], rsB[:]))
                vsep()
                b.v(lambda h: h.tensor_mul(rsC[:], rsC[:], rsA[:]))
                vsep()
                b.v(lambda h: h.tensor_scalar(rsC[:], rsC[:], -0.5, 1.5, ALU.mult,
                                              ALU.add))
                vsep()
                b.v(lambda h: h.tensor_mul(rsB[:], rsB[:], rsC[:]))
                vsep()
            b.v(lambda h: h.tensor_copy(dst, rsB[:, 0:k]))
            vsep()

        tc1 = sb("tc1", [P, 4], F32)
        rsA = sb("rsA", [P, 64], F32)
        rsB = sb("rsB", [P, 64], F32)
        rsC = sb("rsC", [P, 64], F32)
        sepA = sb("sepA", [P, 64], F32)
        colY = sb("colY", [P, 24], F32)
        b.v(lambda h: h.tensor_scalar(tc1[:, 0:1], invt[:], 1e-6, None, ALU.add),
            W(("sync", ld["temp"])))
        vsep()
        rsqrt_newton(tc1[:, 1:2], tc1[:, 0:1], 1)
        vsep()
        b.v(lambda h: h.tensor_mul(invt[:], tc1[:, 1:2], tc1[:, 1:2]))
        b.v(lambda h: h.tensor_sub(dknot[:], knots[:, 1:8], knots[:, 0:7]),
            W(("sync", ld["knots"])))

        load_stg("gf", di["gf_rep"], D)
        stg_free[0] = b.v(lambda h: h.tensor_scalar(g05[:], stg[:, 0:D], 0.5, None,
                                                    ALU.mult), W(("sync", ld["gf"])))
        load_stg("bf", di["bf_rep"], D)
        stg_free[0] = b.v(lambda h: h.tensor_scalar(bt05[:], stg[:, 0:D], 0.5, None,
                                                    ALU.mult), W(("sync", ld["bf"])))
        load_stg("dirs_f", di["dirs"], D, rows=64)
        stg_free[0] = b.v(lambda h: h.tensor_copy(dirs_b[:], stg[:64, 0:D]),
                          W(("sync", ld["dirs_f"])))
        load_stg("ptab_f", di["ptab"], 65, rows=64)
        stg_free[0] = b.v(lambda h: h.tensor_copy(ptab_b[:], stg[:64, 0:65]),
                          W(("sync", ld["ptab_f"])))
        for c in range(8):
            load_stg(f"cw1_{c}", di["Cw1"][c], CH)
            stg_free[0] = b.v(lambda h, c=c: h.tensor_copy(cw1[:, c, :], stg[:, 0:CH]),
                              W(("sync", ld[f"cw1_{c}"])))
        for c in range(2):
            load_stg(f"cw2_{c}", di["Cw2"][c], 2)
            stg_free[0] = b.v(lambda h, c=c: h.tensor_copy(cw2[:, c, :], stg[:, 0:2]),
                              W(("sync", ld[f"cw2_{c}"])))
        for hhf in range(2):
            load_stg(f"binr{hhf}",
                     di["b_in_rep"][:, hhf * (D3 // 2):(hhf + 1) * (D3 // 2)],
                     D3 // 2)
            stg_free[0] = b.v(lambda h, hhf=hhf: h.tensor_copy(
                biasr[:, hhf * (D3 // 2):(hhf + 1) * (D3 // 2)], stg[:]),
                W(("sync", ld[f"binr{hhf}"])))
        gcols = sb("gcols", [P, 8], F32)
        load("gcols", gcols[:], di["gamma_cols"])

        H3 = D3 // 2
        for c in range(8):
            for hhf in range(2):
                load_stg(f"win_{c}_{hhf}", di["W_in"][c][:, hhf * H3:(hhf + 1) * H3],
                         H3)
                wv = b.v(lambda h, c=c, hhf=hhf: h.tensor_copy(
                    winp[:, c, hhf * H3:(hhf + 1) * H3], stg[:]),
                    W(("sync", ld[f"win_{c}_{hhf}"]), ("sync", ld["beta"])))
                for n3 in range(3):
                    n6 = hhf * 3 + n3
                    pdst = (pp0, pp1, pp2)[n6 // 2]
                    off = (n6 % 2) * 512
                    b.t(lambda h, c=c, n6=n6, pdst=pdst, off=off: h.matmul(
                        pdst[:, off:off + 512], zhiT[:, c, :],
                        winp[:, c, n6 * 512:(n6 + 1) * 512],
                        start=(c == 0), stop=(c == 7)),
                        W(("vector", wv)), inc=(n3 == 2))
                stg_free[0] = b.v(lambda h, c=c, hhf=hhf: h.tensor_scalar(
                    winp[:, c, hhf * H3:(hhf + 1) * H3], stg[:], gcols[:, c:c + 1],
                    None, ALU.mult),
                    W(("tensor", b.cnt["tensor"]), ("sync", ld["gcols"])))
        bias_mm = b.cnt["tensor"]
        for n6 in range(6):
            pdst = (pp0, pp1, pp2)[n6 // 2]
            off = (n6 % 2) * 512
            b.v(lambda h, n6=n6, pdst=pdst, off=off: h.tensor_add(
                biasr[:, n6 * 512:(n6 + 1) * 512], biasr[:, n6 * 512:(n6 + 1) * 512],
                pdst[:, off:off + 512]), W(("tensor", bias_mm)))
        pp_free = b.cnt["vector"]

        for c in range(8):
            load_stg(f"wo_{c}", di["W_out"][c], D)
            stg_free[0] = b.v(lambda h, c=c: h.tensor_copy(wout[:, c, :], stg[:, 0:D]),
                              W(("sync", ld[f"wo_{c}"])))

        kT = zloT
        kch = zb[:, 0:P]
        for kk, src in ((0, "k0c"), (1, "k1c")):
            for c in range(8):
                load_stg(f"k{kk}_{c}", di[src][c], P)
                cv = b.v(lambda h: h.tensor_copy(kch, stg[:, 0:P]),
                         W(("sync", ld[f"k{kk}_{c}"])))
                stg_free[0] = cv
                b.t(lambda h, c=c: h.transpose(ppT[:, c * P:(c + 1) * P], kch,
                                               ident[:]), W(("vector", cv)))
                b.v(lambda h, c=c: h.tensor_copy(kT[:, c, :], ppT[:, c * P:(c + 1) * P]),
                    W(("tensor", b.cnt["tensor"])))
            kt_done = b.cnt["vector"]
            for n2 in range(2):
                b.t(lambda h, n2=n2: h.matmul(
                    pp0[:, n2 * 512:(n2 + 1) * 512], fwd_re[:, 0, :],
                    kT[:].rearrange("p c k -> p (c k)")[:, n2 * 512:(n2 + 1) * 512],
                    start=True, stop=True),
                    W(("vector", max(kt_done, pp_free)), ("sync", ld["fwd_re"]),
                      ("sync", ld["fwd_imp"])) if n2 == 0 else (), inc=False)
            for n2 in range(2):
                b.t(lambda h, n2=n2: h.matmul(
                    pp1[:, n2 * 512:(n2 + 1) * 512], fwd_imp[:, 0, :],
                    kT[:].rearrange("p c k -> p (c k)")[:, n2 * 512:(n2 + 1) * 512],
                    start=True, stop=True), inc=(n2 == 1))
            tmm = b.cnt["tensor"]
            kt = ktab[kk]
            b.v(lambda h, kt=kt: h.tensor_copy(kt["A1"][:], pp0[:]), W(("tensor", tmm)))
            b.v(lambda h, kt=kt: h.tensor_copy(kt["B"][:], pp1[:]))
            b.v(lambda h, kt=kt: h.tensor_copy(kt["A2"][:], pp0[:]))
            b.v(lambda h, kt=kt: h.tensor_copy(kt["A2"][0:1, :], pp1[0:1, :]))
            pp_free = b.v(lambda h, kt=kt: h.memset(kt["B"][0:1, :], 0.0))

        state = dict(vready={}, uready={}, x2v={})

        def ln_stats(subs, c0, xw=None):
            n = len(subs)
            for i, s in enumerate(subs):
                sl = s % 10
                w = W(("sync", xw[s])) if xw else ()
                b.v(lambda h, i=i, sl=sl: h.bn_stats(statsA[:, i, 0:6],
                                                     x[:, sl, 0:512]), w)
                b.v(lambda h, i=i, sl=sl: h.bn_stats(statsA[:, i, 6:12],
                                                     x[:, sl, 512:D]))
            vsep()
            vsep()
            st4 = statsA[:, 0:n, :].rearrange("p s (t e) -> p s t e", t=4)
            mu4 = st4[:, :, :, 1]
            M4 = st4[:, :, :, 2]
            b.v(lambda h, n=n: h.tensor_reduce(colA[:, 0:n, c0 + 2], mu4, AXX,
                                               ALU.add))
            b.v(lambda h, n=n: h.tensor_mul(gw1[:, 0:n, 0:4], mu4, mu4))
            vsep()
            b.v(lambda h, n=n: h.tensor_reduce(colA[:, 0:n, c0 + 3],
                                               gw1[:, 0:n, 0:4], AXX, ALU.add))
            b.v(lambda h, n=n: h.tensor_reduce(colA[:, 0:n, c0 + 4], M4, AXX,
                                               ALU.add))
            b.v(lambda h, n=n: h.tensor_scalar(colA[:, 0:n, c0 + 2],
                                               colA[:, 0:n, c0 + 2], 0.25, None,
                                               ALU.mult))
            vsep()
            b.v(lambda h, n=n: h.tensor_scalar(colA[:, 0:n, c0 + 3],
                                               colA[:, 0:n, c0 + 3], 0.25, None,
                                               ALU.mult))
            vsep()
            b.v(lambda h, n=n: h.scalar_tensor_tensor(
                colA[:, 0:n, c0 + 3], colA[:, 0:n, c0 + 4], 1.0 / D,
                colA[:, 0:n, c0 + 3], ALU.mult, ALU.add))
            b.v(lambda h, n=n: h.tensor_mul(colA[:, 0:n, c0 + 4],
                                            colA[:, 0:n, c0 + 2],
                                            colA[:, 0:n, c0 + 2]))
            vsep()
            b.v(lambda h, n=n: h.tensor_sub(colA[:, 0:n, c0 + 3],
                                            colA[:, 0:n, c0 + 3],
                                            colA[:, 0:n, c0 + 4]))
            vsep()
            b.v(lambda h, n=n: h.tensor_scalar(colA[:, 0:n, c0 + 3],
                                               colA[:, 0:n, c0 + 3], EPS, None,
                                               ALU.add))
            vsep()
            rsqrt_newton(colA[:, 0:n, c0 + 1], colA[:, 0:n, c0 + 3], n)
            vsep()
            b.v(lambda h, n=n: h.tensor_mul(colA[:, 0:n, c0 + 0], colA[:, 0:n, c0 + 2],
                                            colA[:, 0:n, c0 + 1]))
            vsep()
            b.v(lambda h, n=n: h.tensor_scalar(colA[:, 0:n, c0 + 0],
                                               colA[:, 0:n, c0 + 0], -1.0, None,
                                               ALU.mult))

        def sigma_chain(dst, src, n):
            b.s(lambda h: h.activation(gw1[:, 0:n, :], src, AF.Abs, scale=invt[:]),
                W(("vector", b.cnt["vector"])))
            ssep()
            sS = b.s(lambda h: h.activation(gw1[:, 0:n, :], gw1[:, 0:n, :], AF.Exp,
                                            scale=-1.0))
            b.v(lambda h: h.tensor_scalar(gw1[:, 0:n, :], gw1[:, 0:n, :], 1.0, None,
                                          ALU.add), W(("scalar", sS)))
            vsep()
            b.v(lambda h: h.tensor_scalar(gw2[:, 0:n, :], gw1[:, 0:n, :], -8.0 / 17.0,
                                          24.0 / 17.0, ALU.mult, ALU.add))
            vsep()
            for _ in range(3):
                b.v(lambda h: h.tensor_mul(dst, gw1[:, 0:n, :], gw2[:, 0:n, :]))
                vsep()
                b.v(lambda h: h.tensor_scalar(dst, dst, -1.0, 2.0, ALU.mult, ALU.add))
                vsep()
                b.v(lambda h: h.tensor_mul(gw2[:, 0:n, :], gw2[:, 0:n, :], dst))
                vsep()
            b.v(lambda h: h.tensor_scalar(gw1[:, 0:n, :], src, 0.0, None, ALU.is_lt))
            b.v(lambda h: h.tensor_scalar(dst, gw2[:, 0:n, :], -2.0, 1.0, ALU.mult,
                                          ALU.add))
            vsep()
            b.v(lambda h: h.tensor_mul(gw1[:, 0:n, :], gw1[:, 0:n, :], dst))
            vsep()
            b.v(lambda h: h.tensor_add(dst, gw2[:, 0:n, :], gw1[:, 0:n, :]))

        def calib(dst, sig, n):
            b.v(lambda h: h.tensor_scalar(dst, sig, 0.0, knots[:, 0:1], ALU.mult,
                                          ALU.add))
            for j in range(7):
                b.v(lambda h, j=j: h.tensor_scalar(gw2[:, 0:n, :], sig, 7.0, -float(j),
                                                   ALU.mult, ALU.add))
                vsep()
                b.v(lambda h: h.tensor_scalar(gw2[:, 0:n, :], gw2[:, 0:n, :], 0.0, 1.0,
                                              ALU.max, ALU.min))
                vsep()
                b.v(lambda h, j=j: h.scalar_tensor_tensor(dst, gw2[:, 0:n, :],
                                                          dknot[:, j:j + 1], dst,
                                                          ALU.mult, ALU.add))
                vsep()

        def argmax8(idx_dst, vals, n, mx_dst):
            b.v(lambda h: h.tensor_reduce(mx_dst, vals, AXX, ALU.max))
            vsep()
            b.v(lambda h, n=n: h.tensor_tensor(gw1[:, 0:n, :], vals,
                                               mx_dst.broadcast_to([P, n, 8]),
                                               ALU.is_equal))
            vsep()
            b.v(lambda h, n=n: h.scalar_tensor_tensor(
                gw1[:, 0:n, :], gw1[:, 0:n, :], -BIG,
                io8B[:].unsqueeze(1).broadcast_to([P, n, 8]), ALU.mult, ALU.add))
            vsep()
            b.v(lambda h: h.tensor_reduce(idx_dst, gw1[:, 0:n, :], AXX, ALU.min))
            vsep()

        def emit_ffn(subs, dma_out=False, xw=None, tap=None):
            n = len(subs)
            ln_stats(subs, 0, xw=xw)
            statv = b.cnt["vector"]
            tprev = [b.cnt["tensor"]]
            vprev = [statv]
            for i, s in enumerate(subs):
                sl = s % 10
                sz = b.s(lambda h, i=i, sl=sl: h.activation(
                    zf[:], x[:, sl, :], AF.Identity, bias=colA[:, i, 0:1],
                    scale=colA[:, i, 1:2]),
                    W(("vector", vprev[0]), ("tensor", tprev[0])))
                vz = b.v(lambda h: h.tensor_copy(zb[:], zf[:]), W(("scalar", sz)))
                vlo = b.v(lambda h: h.tensor_sub(zlo[:], zf[:], zb[:]))
                vprev[0] = vlo
                for c in range(8):
                    b.t(lambda h, c=c: h.transpose(ppT[:, c * P:(c + 1) * P],
                                                   zb[:, c * P:(c + 1) * P], ident[:]),
                        W(("vector", vz)), inc=(c == 7))
                tprev[0] = b.cnt["tensor"]
                e1 = b.v(lambda h: h.tensor_copy(
                    zhiT[:].rearrange("p c k -> p (c k)"), ppT[:]),
                    W(("tensor", b.cnt["tensor"])))
                for c in range(8):
                    b.t(lambda h, c=c: h.transpose(ppT[:, c * P:(c + 1) * P],
                                                   zlo[:, c * P:(c + 1) * P], ident[:]),
                        W(("vector", e1)), inc=(c == 7))
                e2 = b.v(lambda h: h.tensor_copy(
                    zloT[:].rearrange("p c k -> p (c k)"), ppT[:]),
                    W(("tensor", b.cnt["tensor"])))
                for c in range(8):
                    b.t(lambda h, c=c: h.matmul(pps[:, 0:72], zhiT[:, c, :],
                                                sigc[:, c, :], start=(c == 0),
                                                stop=False),
                        W(("vector", e2)) if c == 0 else (), inc=False)
                    b.t(lambda h, c=c: h.matmul(pp2[:, 0:CH], zhiT[:, c, :],
                                                cw1[:, c, :], start=(c == 0),
                                                stop=False), inc=False)
                for c in range(8):
                    b.t(lambda h, c=c: h.matmul(pps[:, 0:72], zloT[:, c, :],
                                                sigc[:, c, :], start=False,
                                                stop=(c == 7)), inc=(c == 7))
                mm_done = b.cnt["tensor"]
                hbias = b.t(lambda h: h.matmul(pp2[:, 0:CH], ones_row[:], cb1r[:],
                                               start=False, stop=True),
                            W(("sync", ld["ones_row"]), ("sync", ld["cb1"])))
                sv = b.v(lambda h, i=i: h.tensor_copy(scoresA[:, i, :], pps[:, 0:72]),
                         W(("tensor", mm_done)))
                sh = b.s(lambda h: h.activation(hb[:], pp2[:, 0:CH], AF.Gelu),
                         W(("tensor", hbias)))
                for c in range(2):
                    b.t(lambda h, c=c: h.transpose(ppT[:, c * P:(c + 1) * P],
                                                   hb[:, c * P:(c + 1) * P], ident[:]),
                        W(("scalar", sh), ("vector", sv)), inc=(c == 1))
                e3 = b.v(lambda h: h.tensor_copy(
                    hT[:].rearrange("p c k -> p (c k)"), ppT[:, 0:2 * P]),
                    W(("tensor", b.cnt["tensor"])))
                for c in range(2):
                    b.t(lambda h, c=c: h.matmul(pps[:, 256:258], hT[:, c, :],
                                                cw2[:, c, :], start=(c == 0),
                                                stop=(c == 1)),
                        W(("vector", e3)) if c == 0 else (), inc=(c == 1))
                tprev[0] = b.cnt["tensor"]
                b.v(lambda h, i=i: h.tensor_add(compA[:, i, :], pps[:, 256:258],
                                                cb2r[:]), W(("tensor", tprev[0])))
            b.s(lambda h, n=n: h.activation(compA[:, 0:n, :], compA[:, 0:n, :],
                                            AF.Tanh), W(("vector", b.cnt["vector"])))
            tanh_done = b.cnt["scalar"]
            sigma_chain(gw0[:, 0:n, 0:8], scoresA[:, 0:n, 64:72], n)
            calib(gates[:, 0:n, 0:8], gw0[:, 0:n, 0:8], n)
            argmax8(rcol[:, 0:n, 0:1], gates[:, 0:n, 0:8], n, rcol[:, 0:n, 5:6])
            b.v(lambda h, n=n: h.tensor_scalar(spw[:, 0:n, 0:8], spw[:, 0:n, 0:8],
                                               0.0, None, ALU.mult))
            for c in range(8):
                b.v(lambda h, n=n, c=c: h.tensor_scalar(gw3[:, 0:n, 0:1],
                                                        rcol[:, 0:n, 0:1], float(c),
                                                        None, ALU.is_equal))
                vsep()
                b.v(lambda h, n=n, c=c: h.tensor_tensor(
                    gw2[:, 0:n, :], scoresA[:, 0:n, c * 8:(c + 1) * 8],
                    gw3[:, 0:n, 0:1].broadcast_to([P, n, 8]), ALU.mult))
                vsep()
                b.v(lambda h, n=n: h.tensor_add(spw[:, 0:n, 0:8], spw[:, 0:n, 0:8],
                                                gw2[:, 0:n, :]))
                vsep()
            sigma_chain(gw0[:, 0:n, 8:16], spw[:, 0:n, 0:8], n)
            calib(gates[:, 0:n, 8:16], gw0[:, 0:n, 8:16], n)
            argmax8(rcol[:, 0:n, 1:2], gates[:, 0:n, 8:16], n, rcol[:, 0:n, 5:6])
            b.v(lambda h, n=n: h.scalar_tensor_tensor(rcol[:, 0:n, 2:3],
                                                      rcol[:, 0:n, 0:1], 8.0,
                                                      rcol[:, 0:n, 1:2], ALU.mult,
                                                      ALU.add))
            vsep()
            b.v(lambda h, n=n: h.tensor_tensor(
                onehot[:, 0:n, :], io64[:].unsqueeze(1).broadcast_to([P, n, 64]),
                rcol[:, 0:n, 2:3].broadcast_to([P, n, 64]), ALU.is_equal))
            oh_done = b.cnt["vector"]
            for i, s in enumerate(subs):
                tt = b.t(lambda h, i=i: h.transpose(ppT[0:64, 0:P], onehot[:, i, :],
                                                    ident[:]),
                         W(("vector", oh_done)), inc=True)
                ev = b.v(lambda h: h.tensor_copy(ohT[:], ppT[0:64, 0:P]),
                         W(("tensor", tt)))
                pm = b.t(lambda h: h.matmul(pps[:, 0:65], ohT[:], ptab_b[:],
                                            start=True, stop=True),
                         W(("vector", ev)))
                b.v(lambda h, i=i: h.tensor_copy(parsA[:, i, :], pps[:, 0:65]),
                    W(("tensor", pm)))
            b.v(lambda h, n=n: h.tensor_tensor(
                spw[:, 0:n, :], parsA[:, 0:n, 0:16],
                compA[:, 0:n, 0:1].broadcast_to([P, n, 16]), ALU.mult),
                W(("scalar", tanh_done)))
            b.v(lambda h, n=n: h.tensor_tensor(
                spw2[:, 0:n, :], parsA[:, 0:n, 16:32],
                compA[:, 0:n, 1:2].broadcast_to([P, n, 16]), ALU.mult))
            b.v(lambda h, n=n: h.tensor_add(spw[:, 0:n, :], spw[:, 0:n, :],
                                            spw2[:, 0:n, :]))
            b.v(lambda h, n=n: h.tensor_add(spw[:, 0:n, :], spw[:, 0:n, :],
                                            parsA[:, 0:n, 32:48]))
            b.v(lambda h, n=n: h.tensor_relu(spw[:, 0:n, :], spw[:, 0:n, :]))
            b.v(lambda h, n=n: h.tensor_mul(spw[:, 0:n, :], spw[:, 0:n, :],
                                            parsA[:, 0:n, 48:64]))
            b.v(lambda h, n=n: h.tensor_reduce(rcol[:, 0:n, 3:4], spw[:, 0:n, :],
                                               AXX, ALU.add))
            vsep()
            b.v(lambda h, n=n: h.tensor_add(rcol[:, 0:n, 3:4], rcol[:, 0:n, 3:4],
                                            parsA[:, 0:n, 64:65]))
            vsep()
            b.v(lambda h, n=n: h.tensor_scalar(rcol[:, 0:n, 4:5], rcol[:, 0:n, 3:4],
                                               oscr[:], None, ALU.mult),
                W(("sync", ld["osc"])))
            vsep()
            mag_done = b.cnt["vector"]
            if tap == "x1":
                for i in range(n):
                    b.d(lambda h, i=i: h.dma_start(out=dbg["d_gates"][i],
                                                   in_=gates[:, i, :]),
                        W(("vector", mag_done)))
                    b.d(lambda h, i=i: h.dma_start(out=dbg["d_rcol"][i],
                                                   in_=rcol[:, i, :]))
                b.d(lambda h: h.dma_start(out=dbg["d_cols"],
                                          in_=colA[:].rearrange("p s k -> p (s k)")),
                    W(("vector", mag_done)))
                for i in range(n):
                    b.d(lambda h, i=i: h.dma_start(out=dbg["d_scores"][i],
                                                   in_=scoresA[:, i, :]),
                        W(("vector", mag_done)))
                    b.d(lambda h, i=i: h.dma_start(out=dbg["d_sig"][i],
                                                   in_=gw0[:, i, :]))
                    b.d(lambda h, i=i: h.dma_start(out=dbg["d_jsel"][i],
                                                   in_=rcol[:, i, 2:3]))
                    b.d(lambda h, i=i: h.dma_start(out=dbg["d_mag"][i],
                                                   in_=rcol[:, i, 3:4]))
                    b.d(lambda h, i=i: h.dma_start(out=dbg["d_comp"][i],
                                                   in_=compA[:, i, :]))
                    b.d(lambda h, i=i: h.dma_start(out=dbg["d_pars"][i],
                                                   in_=parsA[:, i, :]))
            yprev = [mag_done]
            for p0 in range(0, n, 2):
                pair = [(p0, subs[p0]), (p0 + 1, subs[p0 + 1])]
                ybufs = [ybuf, ybuf2]
                yvs = []
                for k2, (i, s) in enumerate(pair):
                    sl = s % 10
                    tt = b.t(lambda h, i=i: h.transpose(ppT[0:64, 0:P],
                                                        onehot[:, i, :], ident[:]),
                             W(("vector", max(mag_done, yprev[0]))), inc=True)
                    ev = b.v(lambda h: h.tensor_copy(ohT[:], ppT[0:64, 0:P]),
                             W(("tensor", tt)))
                    for n2 in range(2):
                        b.t(lambda h, n2=n2: h.matmul(
                            pp0[:, n2 * 512:(n2 + 1) * 512], ohT[:],
                            dirs_b[:, n2 * 512:(n2 + 1) * 512],
                            start=True, stop=True),
                            W(("vector", ev)) if n2 == 0 else (), inc=(n2 == 1))
                    dmm = b.cnt["tensor"]
                    sz = b.s(lambda h, i=i, sl=sl: h.activation(
                        zf[:], x[:, sl, :], AF.Identity, bias=colA[:, i, 0:1],
                        scale=colA[:, i, 1:2]), W(("vector", yprev[0])))
                    yv = b.v(lambda h, i=i, k2=k2: h.scalar_tensor_tensor(
                        ybufs[k2][:], pp0[:], rcol[:, i, 4:5], zf[:], ALU.mult,
                        ALU.add), W(("tensor", dmm), ("scalar", sz)))
                    b.v(lambda h, i=i, k2=k2: h.bn_stats(statsA[:, k2, 0:6],
                                                         ybufs[k2][:, 0:512]))
                    yv2 = b.v(lambda h, i=i, k2=k2: h.bn_stats(
                        statsA[:, k2, 6:12], ybufs[k2][:, 512:D]))
                    yprev[0] = yv2
                    yvs.append(yv)
                vsep()
                vsep()
                stA = statsA[:, 0, :].rearrange("p (t e) -> p t e", t=4)
                stB = statsA[:, 1, :].rearrange("p (t e) -> p t e", t=4)
                b.v(lambda h: h.tensor_reduce(colY[:, 0:1], stA[:, :, 1], AXX, ALU.add))
                b.v(lambda h: h.tensor_reduce(colY[:, 1:2], stB[:, :, 1], AXX, ALU.add))
                b.v(lambda h: h.tensor_mul(colY[:, 10:14], stA[:, :, 1], stA[:, :, 1]))
                b.v(lambda h: h.tensor_mul(colY[:, 14:18], stB[:, :, 1], stB[:, :, 1]))
                b.v(lambda h: h.tensor_reduce(colY[:, 2:3], colY[:, 10:14], AXX, ALU.add))
                b.v(lambda h: h.tensor_reduce(colY[:, 3:4], colY[:, 14:18], AXX, ALU.add))
                b.v(lambda h: h.tensor_reduce(colY[:, 4:5], stA[:, :, 2], AXX, ALU.add))
                b.v(lambda h: h.tensor_reduce(colY[:, 5:6], stB[:, :, 2], AXX, ALU.add))
                b.v(lambda h: h.tensor_scalar(colY[:, 0:1], colY[:, 0:1], -0.25, None,
                                              ALU.mult))
                b.v(lambda h: h.tensor_scalar(colY[:, 1:2], colY[:, 1:2], -0.25, None,
                                              ALU.mult))
                b.v(lambda h: h.tensor_scalar(colY[:, 2:3], colY[:, 2:3], 0.25, None,
                                              ALU.mult))
                b.v(lambda h: h.tensor_scalar(colY[:, 3:4], colY[:, 3:4], 0.25, None,
                                              ALU.mult))
                b.v(lambda h: h.scalar_tensor_tensor(colY[:, 2:3], colY[:, 4:5],
                                                     1.0 / D, colY[:, 2:3], ALU.mult,
                                                     ALU.add))
                b.v(lambda h: h.scalar_tensor_tensor(colY[:, 3:4], colY[:, 5:6],
                                                     1.0 / D, colY[:, 3:4], ALU.mult,
                                                     ALU.add))
                b.v(lambda h: h.tensor_mul(colY[:, 4:5], colY[:, 0:1], colY[:, 0:1]))
                b.v(lambda h: h.tensor_mul(colY[:, 5:6], colY[:, 1:2], colY[:, 1:2]))
                b.v(lambda h: h.tensor_sub(colY[:, 2:3], colY[:, 2:3], colY[:, 4:5]))
                b.v(lambda h: h.tensor_sub(colY[:, 3:4], colY[:, 3:4], colY[:, 5:6]))
                b.v(lambda h: h.tensor_scalar(colY[:, 2:3], colY[:, 2:3], EPS, None,
                                              ALU.add))
                b.v(lambda h: h.tensor_scalar(colY[:, 3:4], colY[:, 3:4], EPS, None,
                                              ALU.add))
                vsep()
                rsqrt_newton(colY[:, 6:8], colY[:, 2:4], 2)
                b.v(lambda h: h.tensor_mul(colY[:, 8:9], colY[:, 0:1], colY[:, 6:7]))
                vr = b.v(lambda h: h.tensor_mul(colY[:, 9:10], colY[:, 1:2],
                                                colY[:, 7:8]))
                for k2, (i, s) in enumerate(pair):
                    sl = s % 10
                    sl2 = b.s(lambda h, k2=k2: h.activation(
                        zf[:], ybufs[k2][:], AF.Identity, bias=colY[:, 8 + k2:9 + k2],
                        scale=colY[:, 6 + k2:7 + k2]),
                        W(("vector", vr if k2 == 0 else yprev[0])))
                    b.v(lambda h: h.tensor_mul(zf[:], zf[:], g05[:]),
                        W(("scalar", sl2)))
                    b.v(lambda h: h.tensor_add(zf[:], zf[:], bt05[:]))
                    xu = b.v(lambda h, sl=sl: h.tensor_add(x[:, sl, :], x[:, sl, :],
                                                           zf[:]))
                    state["x2v"][s] = xu
                    yprev[0] = xu
                    if dma_out:
                        b.d(lambda h, s=s, sl=sl: h.dma_start(out=xout[s - 2],
                                                              in_=x[:, sl, :]),
                            W(("vector", xu)))
                    if tap == "x1":
                        b.d(lambda h, i=i, sl=sl: h.dma_start(out=dbg["d_x1"][i],
                                                              in_=x[:, sl, :]),
                            W(("vector", xu)))
                    if tap == "x2o":
                        b.d(lambda h, i=i, sl=sl: h.dma_start(out=dbg["d_x2"][i],
                                                              in_=x[:, sl, :]),
                            W(("vector", xu)))

        def conv_window(s, ring, vwait, kt, out_cb, px1, px2, pinv):
            sw = state.get("spec_evac", 0)
            for mt, mats, pdst in ((0, fwd_re, px1), (1, fwd_imp, px2)):
                for kc, wslot in ((0, (s - 1) % 2), (1, s % 2)):
                    for n2 in range(2):
                        b.t(lambda h, mats=mats, kc=kc, n2=n2, pdst=pdst, wslot=wslot:
                            h.matmul(pdst[:, n2 * 512:(n2 + 1) * 512], mats[:, kc, :],
                                     ring[:, wslot, n2 * 512:(n2 + 1) * 512],
                                     start=(kc == 0), stop=(kc == 1)),
                            W(("vector", vwait), ("scalar", sw))
                            if (mt == 0 and kc == 0 and n2 == 0)
                            else (), inc=(mt == 1 and kc == 1 and n2 == 1))
            fwd_done = b.cnt["tensor"]
            se1 = b.s(lambda h, px1=px1: h.activation(fxr, px1[:], AF.Identity),
                      W(("tensor", fwd_done)))
            se2 = b.s(lambda h, px2=px2: h.activation(fxi, px2[:], AF.Identity))
            state["spec_evac"] = se2
            b.v(lambda h, kt=kt: h.tensor_tensor(fq1[:], fxr, kt["A1"][:],
                                                 ALU.mult),
                W(("scalar", se1)))
            b.v(lambda h, kt=kt: h.tensor_tensor(fqt[:], fxi, kt["B"][:], ALU.mult),
                W(("scalar", se2)))
            b.v(lambda h: h.tensor_sub(fq1[:], fq1[:], fqt[:]))
            b.v(lambda h, kt=kt: h.tensor_tensor(fq2[:], fxr, kt["B"][:], ALU.mult))
            b.v(lambda h, kt=kt: h.tensor_tensor(fqt[:], fxi, kt["A2"][:],
                                                 ALU.mult))
            fm = b.v(lambda h: h.tensor_add(fq2[:], fq2[:], fqt[:]))
            for n2 in range(2):
                b.t(lambda h, n2=n2, pinv=pinv: h.matmul(
                    pinv[:, n2 * 512:(n2 + 1) * 512],
                    inv_c[:], fq1[:, n2 * 512:(n2 + 1) * 512],
                    start=True, stop=False),
                    W(("vector", fm), ("scalar", se1)) if n2 == 0 else (), inc=False)
            for n2 in range(2):
                b.t(lambda h, n2=n2, pinv=pinv: h.matmul(
                    pinv[:, n2 * 512:(n2 + 1) * 512],
                    inv_s[:], fq2[:, n2 * 512:(n2 + 1) * 512],
                    start=False, stop=True), inc=(n2 == 1))
            return b.v(out_cb, W(("tensor", b.cnt["tensor"])))

        def emit_attn(subs, tap=None):
            n = len(subs)
            ln_stats(subs, 8)
            b.v(lambda h, n=n: h.tensor_scalar(colA[:, 0:n, 11], colA[:, 0:n, 11],
                                               -EPS, None, ALU.add))
            b.v(lambda h, n=n: h.tensor_mul(colA[:, 0:n, 12], colA[:, 0:n, 9],
                                            colA[:, 0:n, 9]))
            vsep()
            b.v(lambda h, n=n: h.tensor_mul(colA[:, 0:n, 12], colA[:, 0:n, 12],
                                            colA[:, 0:n, 11]))
            vsep()
            b.v(lambda h, n=n: h.tensor_scalar(colA[:, 0:n, 12], colA[:, 0:n, 12],
                                               EPS, None, ALU.add))
            vsep()
            rsqrt_newton(colA[:, 0:n, 13], colA[:, 0:n, 12], n)
            vsep()
            b.v(lambda h, n=n: h.tensor_mul(colA[:, 0:n, 10], colA[:, 0:n, 9],
                                            colA[:, 0:n, 13]))
            b.v(lambda h, n=n: h.tensor_mul(colA[:, 0:n, 12], colA[:, 0:n, 8],
                                            colA[:, 0:n, 13]))
            b.v(lambda h, n=n: h.tensor_scalar(colA[:, 0:n, 11], colA[:, 0:n, 9],
                                               1.0, None, ALU.add))
            stat_done = b.cnt["vector"]
            tprev = [b.cnt["tensor"]]
            eprev = [b.cnt["vector"]]
            for i, s in enumerate(subs):
                sl = s % 10
                sz = b.s(lambda h, i=i, sl=sl: h.activation(
                    zb[:], x[:, sl, :], AF.Identity, bias=colA[:, i, 12:13],
                    scale=colA[:, i, 10:11]),
                    W(("vector", stat_done), ("tensor", tprev[0])))
                for c in range(8):
                    b.t(lambda h, c=c: h.transpose(ppT[:, c * P:(c + 1) * P],
                                                   zb[:, c * P:(c + 1) * P],
                                                   ident[:]),
                        W(("scalar", sz), ("vector", eprev[0])), inc=(c == 7))
                tprev[0] = b.cnt["tensor"]
                e1 = b.v(lambda h: h.tensor_copy(
                    zhiT[:].rearrange("p c k -> p (c k)"), ppT[:]),
                    W(("tensor", b.cnt["tensor"])))
                eprev[0] = e1
                slice_evac = state.get("proj_evac", [0, 0])
                for n6 in range(6):
                    off = (n6 % 2) * 512
                    for c in range(8):
                        b.t(lambda h, c=c, n6=n6, off=off: h.matmul(
                            pp0[:, off:off + 512], zhiT[:, c, :],
                            winp[:, c, n6 * 512:(n6 + 1) * 512],
                            start=(c == 0), stop=(c == 7)),
                            W(("vector", max(e1, slice_evac[n6 % 2])))
                            if c == 0 else (), inc=(c == 7))
                    mmc = b.cnt["tensor"]
                    seg, hoff = n6 // 2, (n6 % 2) * 512
                    dst = (vring[:, s % 2, hoff:hoff + 512] if seg == 0 else
                           (g1c if seg == 1 else g2c)[:, hoff:hoff + 512])
                    ev = b.v(lambda h, dst=dst, n6=n6, off=off: h.tensor_add(
                        dst, pp0[:, off:off + 512],
                        biasr[:, n6 * 512:(n6 + 1) * 512]),
                        W(("tensor", mmc)))
                    slice_evac[n6 % 2] = ev
                state["proj_evac"] = slice_evac
                if s < 2:
                    b.v(lambda h, s=s: h.tensor_scalar(vring[:, s % 2, :],
                                                       vring[:, s % 2, :],
                                                       maskc[:, s:s + 1], None,
                                                       ALU.mult),
                        W(("sync", ld["mask"])))
                state["vready"][s] = b.cnt["vector"]
                if tap and s == 2:
                    b.d(lambda h, s=s: h.dma_start(out=dbg["d_v"][0],
                                                   in_=vring[:, s % 2, :]),
                        W(("vector", state["vready"][s])))
                    b.d(lambda h: h.dma_start(out=dbg["d_g1"][0], in_=g1c[:]))
                if s >= 1:
                    uv = conv_window(
                        s, vring, max(state["vready"][s],
                                      state["vready"].get(s - 1, 0)), ktab[0],
                        lambda h, s=s: h.tensor_tensor(uring[:, s % 2, :], pp1[:],
                                                       g1c[:], ALU.mult),
                        pp1, pp2, pp1)
                    state["uready"][s] = uv
                    if tap and s == 2:
                        b.d(lambda h, s=s: h.dma_start(out=dbg["d_u"][0],
                                                       in_=uring[:, s % 2, :]),
                            W(("vector", uv)))
                if s >= 2:
                    yv = conv_window(
                        s, uring, max(state["uready"][s],
                                      state["uready"].get(s - 1, 0)), ktab[1],
                        lambda h: h.tensor_tensor(ygc[:], pp2[:], g2c[:], ALU.mult),
                        pp2, pp1, pp2)
                    if tap and s == 2:
                        b.d(lambda h: h.dma_start(out=dbg["d_yg"][0], in_=ygc[:]),
                            W(("vector", yv)))
                    for c in range(8):
                        b.t(lambda h, c=c: h.transpose(ppT[:, c * P:(c + 1) * P],
                                                       ygc[:, c * P:(c + 1) * P],
                                                       ident[:]),
                            W(("vector", yv)), inc=(c == 7))
                    e2 = b.v(lambda h: h.tensor_copy(
                        zloT[:].rearrange("p c k -> p (c k)"), ppT[:]),
                        W(("tensor", b.cnt["tensor"])))
                    eprev[0] = e2
                    for c in range(8):
                        for n2 in range(2):
                            b.t(lambda h, c=c, n2=n2: h.matmul(
                                pp0[:, n2 * 512:(n2 + 1) * 512], zloT[:, c, :],
                                wout[:, c, n2 * 512:(n2 + 1) * 512],
                                start=(c == 0), stop=(c == 7)),
                                W(("vector", e2)) if (c == 0 and n2 == 0) else (),
                                inc=(c == 7 and n2 == 1))
                    wo_done = b.cnt["tensor"]
                    b.v(lambda h: h.tensor_add(ybuf[:], pp0[:], boutr[:]),
                        W(("tensor", wo_done)))
                    b.v(lambda h, i=i: h.tensor_scalar(ybuf[:], ybuf[:],
                                                       colA[:, i, 8:9], None,
                                                       ALU.add))
                    xu = b.v(lambda h, i=i, sl=sl: h.scalar_tensor_tensor(
                        x[:, sl, :], x[:, sl, :], colA[:, i, 11:12], ybuf[:],
                        ALU.mult, ALU.add))
                    state["x2v"][s] = xu

        emit_ffn(HALF1, xw=x_loaded, tap=("x1" if debug else None))
        emit_attn(HALF1, tap=debug)
        emit_ffn(HALF1[2:], dma_out=True, tap=("x2o" if debug else None))
        x_loaded2 = {}
        reload_wait = b.cnt["sync"]
        for s in HALF2:
            x_loaded2[s] = b.d(
                lambda h, s=s: h.dma_start(out=x[:, s % 10, :], in_=di["xin"][s]),
                W(("vector", b.cnt["vector"]), ("sync", reload_wait)))
        emit_ffn(HALF2, xw=x_loaded2)
        emit_attn(HALF2)
        emit_ffn(HALF2, dma_out=True)

        b.g(lambda h: None, W(("sync", b.cnt["sync"])))
        b.run(block, sems)

    return nc



def _in_maps(inputs):
    f32 = np.float32
    bf16 = ml_dtypes.bfloat16
    x = np.ascontiguousarray(np.asarray(inputs["L0_attn_x"]), dtype=f32)
    W_in = np.ascontiguousarray(np.asarray(inputs["L0_attn_W_in"], f32).reshape(8, P, D3))
    W_out = np.ascontiguousarray(np.asarray(inputs["L0_attn_W_out"], f32).reshape(8, P, D))
    Cw1 = np.ascontiguousarray(np.asarray(inputs["L0_ffn_compress_w1"], f32).reshape(8, P, CH))
    Cw2 = np.ascontiguousarray(np.asarray(inputs["L0_ffn_compress_w2"], f32).reshape(2, P, 2))
    k0c = np.ascontiguousarray(np.asarray(inputs["L0_attn_k0"], f32)[0].reshape(8, P, P))
    k1c = np.ascontiguousarray(np.asarray(inputs["L0_attn_k1"], f32)[0].reshape(8, P, P))
    gamma = np.asarray(inputs["L0_attn_gamma"], f32)
    beta = np.asarray(inputs["L0_attn_beta"], f32)
    gamma_cols = np.ascontiguousarray(gamma.reshape(8, P).T)
    beta_rep = np.ascontiguousarray(
        np.broadcast_to(beta.reshape(8, P, 1), (8, P, P)).transpose(1, 0, 2)
    ).astype(bf16)
    b_in_rep = np.ascontiguousarray(
        np.broadcast_to(np.asarray(inputs["L0_attn_b_in"], f32)[None, :], (P, D3)))
    b_out_rep = np.ascontiguousarray(
        np.broadcast_to(np.asarray(inputs["L0_attn_b_out"], f32)[None, :], (P, D)))
    cb1_row = np.ascontiguousarray(
        np.asarray(inputs["L0_ffn_compress_b1"], f32)[None, :]).astype(bf16)
    cb2_rep = np.ascontiguousarray(
        np.broadcast_to(np.asarray(inputs["L0_ffn_compress_b2"], f32)[None, :], (P, 2)))
    gf_rep = np.ascontiguousarray(
        np.broadcast_to(np.asarray(inputs["L0_ffn_norm_gamma"], f32)[None, :], (P, D)))
    bf_rep = np.ascontiguousarray(
        np.broadcast_to(np.asarray(inputs["L0_ffn_norm_beta"], f32)[None, :], (P, D)))
    dirs = np.ascontiguousarray(np.asarray(inputs["L0_ffn_directions"], f32))
    ptab = np.ascontiguousarray(np.concatenate([
        np.asarray(inputs["L0_ffn_spline_w1"], f32).reshape(64, 32),
        np.asarray(inputs["L0_ffn_spline_b1"], f32),
        np.asarray(inputs["L0_ffn_spline_w2"], f32).reshape(64, 16),
        np.asarray(inputs["L0_ffn_spline_b2"], f32).reshape(64, 1)], axis=1))
    knots_rep = np.ascontiguousarray(
        np.broadcast_to(np.asarray(inputs["L0_ffn_knot_values"], f32)[None, :], (P, 8)))
    temp_rep = np.ascontiguousarray(
        np.broadcast_to(np.asarray(inputs["L0_ffn_temperature"], f32)[None, :], (P, 1)))
    osc_rep = np.ascontiguousarray(
        np.broadcast_to(np.asarray(inputs["L0_ffn_output_scale"], f32)[None, :], (P, 1)))
    fwd_re, fwd_imp, inv_c, inv_s = _dft_tables()
    ident = np.eye(P, dtype=bf16)
    io64 = np.ascontiguousarray(
        np.broadcast_to(np.arange(64, dtype=f32)[None, :], (P, 64)))
    io8B = np.ascontiguousarray(
        np.broadcast_to((np.arange(8).astype(np.float64) + BIG).astype(f32)[None, :],
                        (P, 8)))
    ones_row = np.ones((1, P), bf16)

    shared = dict(W_in=W_in, W_out=W_out, Cw1=Cw1, Cw2=Cw2,
                  sigs_raw=np.ascontiguousarray(np.asarray(inputs["L0_ffn_signatures_raw"], f32)),
                  k0c=k0c, k1c=k1c, gamma_cols=gamma_cols, beta_rep=beta_rep,
                  b_in_rep=b_in_rep, b_out_rep=b_out_rep, cb1_row=cb1_row,
                  cb2_rep=cb2_rep, gf_rep=gf_rep, bf_rep=bf_rep, dirs=dirs,
                  ptab=ptab, knots_rep=knots_rep, temp_rep=temp_rep,
                  osc_rep=osc_rep, fwd_re=fwd_re, fwd_imp=fwd_imp, inv_c=inv_c,
                  inv_s=inv_s, ident=ident, io64=io64, io8B=io8B,
                  ones_row=ones_row)

    maps = []
    for core in range(8):
        bseq, half = core // 2, core % 2
        start = half * 2048
        xin = np.zeros((S_TOT, P, D), f32)
        if half == 1:
            xin[:] = x[bseq, start - 256:start + 2048].reshape(S_TOT, P, D)
        else:
            xin[2:] = x[bseq, start:start + 2048].reshape(16, P, D)
        m = dict(shared)
        m["xin"] = xin
        m["halo_mask"] = np.full((P, 2), 1.0 if half == 1 else 0.0, f32)
        maps.append(m)
    return maps


def kernel(**inputs):
    maps = _in_maps(inputs)
    nc = build_nc(debug=_DEBUG)
    res = run_bass_kernel_spmd(nc, maps, core_ids=list(range(8)))
    out = np.empty((4, 4096, 1024), np.float32)
    for core in range(8):
        bseq, half = core // 2, core % 2
        xo = res.results[core]["xout"].reshape(2048, D)
        out[bseq, half * 2048:(half + 1) * 2048] = xo
    return out
